# version 16
# speedup vs baseline: 1.1149x; 1.0031x over previous
import numpy as np
import ml_dtypes
import concourse.bass as bass
import concourse.mybir as mybir
from concourse.alu_op_type import AluOpType as A
from concourse.bass_utils import run_bass_kernel_spmd
from contextlib import ExitStack

F32 = mybir.dt.float32
BF16 = mybir.dt.bfloat16
AF = mybir.ActivationFunctionType
AX = mybir.AxisListType

H, DK, DV, D, HF = 8, 256, 512, 2048, 5632
T = 2048
NT = 16
NCH = 8
FC = 16
SCALE = DK ** -0.5
ENGINES = ("sync", "gpsimd", "tensor", "vector", "scalar")


def _gammas():
    return 1.0 - 2.0 ** (-5.0 - np.arange(H, dtype=np.float64))


class Sem:
    def __init__(self, h, name):
        self.h, self.name, self.n = h, name, 0


class Prog:
    def __init__(self):
        self.p = {e: [] for e in ENGINES}

    def op(self, eng, method, *args, inc=None, incn=1, **kw):
        if inc is not None:
            inc.n += incn
        semh = inc.h if inc is not None else None

        def f(e, _m=method, _a=args, _k=kw, _s=semh, _n=incn):
            r = getattr(e, _m)(*_a, **_k)
            if _s is not None:
                r.then_inc(_s, _n)
        self.p[eng].append(f)
        return inc.n if inc is not None else None

    def wait(self, eng, sem, val=None):
        v = sem.n if val is None else val
        if v <= 0:
            return
        self.p[eng].append(lambda e, _s=sem.h, _v=v: e.wait_ge(_s, _v))


class Guard:

    def __init__(self):
        self.sem, self.val = None, 0

    def acquire(self, q, eng):
        if self.sem is not None:
            q.wait(eng, self.sem, self.val)

    def release(self, sem):
        self.sem, self.val = sem, sem.n


def build_program(debug_out=()):
    nc = bass.Bass("TRN2", target_bir_lowering=False, debug=False)

    def dram(name, shape, dtype, kind=None):
        if name in debug_out:
            kind = "ExternalOutput"
        if kind is None:
            return nc.dram_tensor(name, shape, dtype)
        return nc.dram_tensor(name, shape, dtype, kind=kind)

    x_d = dram("x", [T, D], F32, "ExternalInput").ap()
    wq_d = dram("wq", [D, D], F32, "ExternalInput").ap()
    wk_d = dram("wk", [D, D], F32, "ExternalInput").ap()
    wv_d = dram("wv", [D, 2 * D], F32, "ExternalInput").ap()
    wg_d = dram("wg", [D, 2 * D], F32, "ExternalInput").ap()
    wo_d = dram("wo", [2 * D, D], F32, "ExternalInput").ap()
    w1s_d = dram("w1s", [HF, 256], F32, "ExternalInput").ap()
    w2s_d = dram("w2s", [HF, 256], F32, "ExternalInput").ap()
    w3s_d = dram("w3s", [D, 704], F32, "ExternalInput").ap()
    cqt_d = dram("cqt", [128, H * T], BF16, "ExternalInput").ap()
    sqt_d = dram("sqt", [128, H * T], BF16, "ExternalInput").ap()
    ckt_d = dram("ckt", [128, H * T], BF16, "ExternalInput").ap()
    skt_d = dram("skt", [128, H * T], BF16, "ExternalInput").ap()
    cmask_d = dram("cmask", [128, 384], F32, "ExternalInput").ap()
    scol_d = dram("scolt", [128, H], F32, "ExternalInput").ap()
    gnc_d = dram("gnc", [128, 4], F32, "ExternalInput").ap()
    id_d = dram("ident", [128, 128], BF16, "ExternalInput").ap()
    y_d = dram("y", [T, D], F32, "ExternalOutput").ap()

    xn_d = dram("xn_h", [FC * 128, T], BF16).ap()
    q_d = dram("q_h", [128, NCH * 16 * 256], BF16).ap()
    k_d = dram("k_h", [128, NCH * 16 * 256], BF16).ap()
    ks_d = dram("ks_h", [T, H * DK], BF16).ap()
    v_d = dram("v_h", [T, H * DV], BF16).ap()
    sg_d = dram("sg_h", [T, H * DV], BF16).ap()
    ot_d = dram("ot_h", [128, NCH * 32 * 256], BF16).ap()
    at_d = dram("at_h", [T, D], BF16).ap()
    gin_d = dram("g_in", [12, 1], F32)
    gout_d = dram("g_out", [12, 1], F32)
    w1i_d = dram("w1qt_in", [256, HF], BF16)
    w2i_d = dram("w2qt_in", [256, HF], BF16)
    w3i_d = dram("w3qt_in", [704, D], BF16)
    w1t_d = dram("w1qt", [D, HF], BF16)
    w2t_d = dram("w2qt", [D, HF], BF16)
    w3t_d = dram("w3qt", [HF, D], BF16)
    si_d = dram("sp_in", [128, H * 1024], BF16)
    w1dbg_d = dram("w1qt_dbg", [D, HF], BF16)
    w1q_raw_d = dram("w1q_raw", [128, 11264], BF16)
    qsc_d = dram("qscale_dbg", [128, 16], F32)
    w1i_dbg_d = dram("w1i_dbg", [256, HF], BF16)
    w3dbg_d = dram("w3qt_dbg", [HF, D], BF16)
    lnst_d = dram("lnstat", [128, 9 * NT], F32)
    so_d = dram("sp_out", [256, H * 1024], BF16)

    q = Prog()
    es = ExitStack()
    E = es.enter_context

    ldK = Sem(E(nc.semaphore("ldK")), "ldK")
    ldA = Sem(E(nc.semaphore("ldA")), "ldA")
    ldB = Sem(E(nc.semaphore("ldB")), "ldB")
    ldC = Sem(E(nc.semaphore("ldC")), "ldC")
    ldD = Sem(E(nc.semaphore("ldD")), "ldD")
    cstA = Sem(E(nc.semaphore("cstA")), "cstA")
    cstB = Sem(E(nc.semaphore("cstB")), "cstB")
    stA = Sem(E(nc.semaphore("stA")), "stA")
    stB = Sem(E(nc.semaphore("stB")), "stB")
    tn = Sem(E(nc.semaphore("tn")), "tn")
    vc = Sem(E(nc.semaphore("vc")), "vc")
    sc = Sem(E(nc.semaphore("sc")), "sc")
    ccg = Sem(E(nc.semaphore("ccg")), "ccg")
    ccw1 = Sem(E(nc.semaphore("ccw1")), "ccw1")
    ccw2 = Sem(E(nc.semaphore("ccw2")), "ccw2")
    ccw3 = Sem(E(nc.semaphore("ccw3")), "ccw3")
    ccs = Sem(E(nc.semaphore("ccs")), "ccs")
    ALLS = [ldK, ldA, ldB, ldC, ldD, cstA, cstB, stA, stB, tn, vc, sc]

    def barrier():
        snap = [(s, s.n) for s in ALLS]
        for eng in ENGINES:
            for s, v in snap:
                q.wait(eng, s, v)

    id_s = E(nc.sbuf_tensor("id_s", [128, 128], BF16))
    gnc_s = E(nc.sbuf_tensor("gnc_s", [128, 4], F32))
    dumf = E(nc.sbuf_tensor("dumf", [128, 2], F32))
    q.op("sync", "dma_start", out=id_s[:, :], in_=id_d, inc=ldK, incn=16)
    q.op("sync", "dma_start", out=gnc_s[:, :], in_=gnc_d, inc=ldK, incn=16)

    rom2 = ExitStack()
    cmask_s = rom2.enter_context(nc.sbuf_tensor("cmask_s", [128, 384], F32))
    scol_s = rom2.enter_context(nc.sbuf_tensor("scol_s", [128, H], F32))
    q.op("sync", "dma_start", out=cmask_s[:, :], in_=cmask_d, inc=ldK, incn=16)
    q.op("sync", "dma_start", out=scol_s[:, :], in_=scol_d, inc=ldK, incn=16)
    consts_loaded = ldK.n
    gscope = ExitStack()
    parts = gscope.enter_context(nc.sbuf_tensor("parts", [128, 12], F32))
    onesc = gscope.enter_context(nc.sbuf_tensor("onesc", [128, 1], F32))
    ones1 = gscope.enter_context(nc.sbuf_tensor("ones1", [1, 128], F32))
    gs3 = gscope.enter_context(nc.sbuf_tensor("gs3", [12, 1], F32))
    gsr = gscope.enter_context(nc.sbuf_tensor("gsr", [1, 12], F32))
    g3 = gscope.enter_context(nc.sbuf_tensor("g3", [1, 3], F32))
    srow = gscope.enter_context(nc.sbuf_tensor("srow", [1, 3], F32))
    scm = gscope.enter_context(nc.sbuf_tensor("scm", [128, 3], F32))
    nscm = gscope.enter_context(nc.sbuf_tensor("nscm", [128, 3], F32))
    q.op("vector", "memset", onesc[:, :], 1.0)
    q.op("vector", "memset", ones1[:, :], 1.0, inc=vc)
    mats = [
        (w1s_d, (44, 256), w1i_d, w1t_d, ccw1, "w12"),
        (w2s_d, (44, 256), w2i_d, w2t_d, ccw2, "w12"),
        (w3s_d, (16, 704), w3i_d, w3t_d, ccw3, "w3"),
    ]

    xnfm_scope = ExitStack()
    xn_fm = xnfm_scope.enter_context(nc.sbuf_tensor("xn_fm", [128, FC * T], BF16))
    xnv = xn_fm[:, :].rearrange("p (fc t) -> p fc t", fc=FC)

    a1a = ExitStack()
    x_s = [a1a.enter_context(nc.sbuf_tensor(f"x_s{i}", [128, D], F32, side="right")) for i in range(2)]
    xn_tm = [a1a.enter_context(nc.sbuf_tensor(f"xn_tm{i}", [128, D], BF16, side="right")) for i in range(2)]
    junkA = a1a.enter_context(nc.sbuf_tensor("junkA", [128, D], BF16, side="right"))
    sums = a1a.enter_context(nc.sbuf_tensor("sums", [128, NT], F32, side="right"))
    ssq = a1a.enter_context(nc.sbuf_tensor("ssq", [128, NT], F32, side="right"))
    sm1 = a1a.enter_context(nc.sbuf_tensor("sm1", [128, 6], F32, side="right"))
    smdbg = a1a.enter_context(nc.sbuf_tensor("smdbg", [128, 6 * NT], F32, side="right"))
    rstdall = a1a.enter_context(nc.sbuf_tensor("rstdall", [128, NT], F32, side="right"))
    bcol = a1a.enter_context(nc.sbuf_tensor("bcol", [128, 1], F32, side="right"))
    ptrA = a1a.enter_context(nc.psum_tensor("ptrA", [128, 1024], BF16))

    xg = [Guard(), Guard()]
    xng = [Guard(), Guard()]
    trg = [Guard(), Guard()]
    xn_ready = [0] * NT
    q.wait("sync", ldK, consts_loaded)
    for t in range(NT):
        bi = t % 2
        ldx = (ldA, ldB)[bi]
        xg[bi].acquire(q, "sync")
        q.op("sync", "dma_start", out=x_s[bi][:, :], in_=x_d[t * 128:(t + 1) * 128, :],
             inc=ldx, incn=16)
        lv = ldx.n
        q.wait("vector", ldx, lv)
        q.op("vector", "tensor_reduce", sums[:, t:t + 1], x_s[bi][:, :], AX.X, A.add, inc=vc)
        q.wait("scalar", ldx, lv)
        q.op("scalar", "activation", junkA[:, :], x_s[bi][:, :], AF.Square,
             accum_out=ssq[:, t:t + 1], inc=sc)
        q.wait("scalar", sc, sc.n)
        q.op("scalar", "copy", dumf[:, 0:1], ssq[:, t:t + 1], inc=sc)
        q.wait("vector", sc, sc.n)
        q.wait("vector", vc, vc.n)
        q.op("vector", "tensor_scalar_mul", sm1[:, 0:1], sums[:, t:t + 1], 1.0 / D, inc=vc)
        q.op("vector", "tensor_scalar", sm1[:, 1:2], ssq[:, t:t + 1], 1.0 / D, 1e-5,
             A.mult, A.add, inc=vc)
        q.wait("vector", vc, vc.n)
        q.op("vector", "tensor_scalar", sm1[:, 2:3], sm1[:, 0:1], sm1[:, 0:1], 0.0,
             A.mult, A.add, inc=vc)
        q.wait("vector", vc, vc.n)
        q.op("vector", "tensor_tensor", sm1[:, 3:4], sm1[:, 1:2], sm1[:, 2:3], A.subtract,
             inc=vc)
        q.wait("scalar", vc, vc.n)
        q.op("scalar", "activation", sm1[:, 4:5], sm1[:, 3:4], AF.Ln, inc=sc)
        q.wait("scalar", sc, sc.n)
        q.op("scalar", "activation", rstdall[:, t:t + 1], sm1[:, 4:5], AF.Exp, scale=-0.5, inc=sc)
        if "lnstat" in debug_out:
            q.wait("vector", sc, sc.n)
            q.op("vector", "tensor_copy", smdbg[:, t * 6:(t + 1) * 6], sm1[:, 0:6], inc=vc)
        q.wait("vector", sc, sc.n)
        q.op("vector", "scalar_tensor_tensor", bcol[:, :], sm1[:, 0:1], -1.0, rstdall[:, t:t + 1],
             A.mult, A.mult, inc=vc)
        xng[bi].acquire(q, "scalar")
        q.wait("scalar", vc, vc.n)
        q.op("scalar", "activation", xn_tm[bi][:, :], x_s[bi][:, :], AF.Identity,
             bias=bcol[:, 0:1], scale=rstdall[:, t:t + 1], inc=sc)
        xg[bi].release(sc)
        q.wait("tensor", sc, sc.n)
        for g in range(2):
            ptr = ptrA
            trg[0].acquire(q, "tensor")
            for j in range(8):
                fc = g * 8 + j
                q.op("tensor", "transpose", ptr[:, j * 128:(j + 1) * 128],
                     xn_tm[bi][:, fc * 128:(fc + 1) * 128], id_s[:, :],
                     inc=(tn if j == 7 else None))
            q.wait("vector", tn, tn.n)
            q.op("vector", "tensor_copy",
                 xnv[:, g * 8:(g + 1) * 8, t * 128:(t + 1) * 128],
                 ptr[:, :].rearrange("p (j t) -> p j t", j=8), inc=vc)
            trg[0].release(vc)
            if g == 1:
                xng[bi].release(vc)
                xn_ready[t] = vc.n
    if "lnstat" in debug_out:
        q.wait("scalar", vc, vc.n)
        q.wait("scalar", sc, sc.n)
        q.op("scalar", "dma_start", out=lnst_d.ap()[:, 0:NT], in_=sums[:, :], inc=stA, incn=16)
        q.op("scalar", "dma_start", out=lnst_d.ap()[:, NT:2 * NT], in_=ssq[:, :], inc=stA, incn=16)
        q.op("scalar", "dma_start", out=lnst_d.ap()[:, 2 * NT:3 * NT], in_=rstdall[:, :], inc=stA, incn=16)
        q.op("scalar", "dma_start", out=lnst_d.ap()[:, 3 * NT:9 * NT], in_=smdbg[:, :], inc=stA, incn=16)
    q.wait("scalar", vc, vc.n)
    for fc in range(FC):
        q.op("scalar", "dma_start", out=xn_d[fc * 128:(fc + 1) * 128, :],
             in_=xnv[:, fc, :], inc=stA, incn=16)

    qk = ExitStack()
    wqp = [qk.enter_context(nc.sbuf_tensor(f"wqp{i}", [128, 16 * 256], BF16, side="right")) for i in range(2)]
    wkp = [qk.enter_context(nc.sbuf_tensor(f"wkp{i}", [128, 16 * 256], BF16, side="right")) for i in range(2)]
    tbl = [qk.enter_context(nc.sbuf_tensor(f"tbl{i}", [128, 4 * T], BF16, side="right")) for i in range(2)]
    qbuf = [qk.enter_context(nc.sbuf_tensor(f"qbuf{i}", [128, 2816], F32, side="right")) for i in range(2)]
    qjunk = qk.enter_context(nc.sbuf_tensor("qjunk", [128, 2816], BF16, side="right"))
    q1s = qk.enter_context(nc.sbuf_tensor("q1s", [128, 1024], BF16, side="right"))
    q2s = qk.enter_context(nc.sbuf_tensor("q2s", [128, 1024], BF16, side="right"))
    rt1 = qk.enter_context(nc.sbuf_tensor("rt1", [128, 1024], BF16, side="right"))
    rt2 = qk.enter_context(nc.sbuf_tensor("rt2", [128, 1024], BF16, side="right"))
    qro = qk.enter_context(nc.sbuf_tensor("qro", [128, 2 * 1024], BF16, side="right"))
    kro = qk.enter_context(nc.sbuf_tensor("kro", [128, 2 * 1024], BF16, side="right"))
    kss = qk.enter_context(nc.sbuf_tensor("kss", [128, 8 * 256], BF16, side="right"))
    pP = [qk.enter_context(nc.psum_tensor(f"pP{i}", [128, 1024], F32)) for i in range(3)]
    ptrK = qk.enter_context(nc.psum_tensor("ptrK", [128, 1024], BF16))

    wg_g = [Guard(), Guard()]
    p1jobs = []
    for m, (dr, (va, vb), *_r) in enumerate(mats):
        qa = va // 4
        for qi in range(4):
            p1jobs.append((m, qi, dr, qa))
    p1_loaded = [None] * 12
    qb_g = [Guard(), Guard()]

    def p1_load(j):
        m, qi, dr, qa = p1jobs[j]
        pbi = j % 2
        qb_g[pbi].acquire(q, "sync")
        srcv = dr.rearrange("(a p) f -> p a f", p=128)
        q.op("sync", "dma_start",
             out=qbuf[pbi][:, :].rearrange("p (a f) -> p a f", a=qa),
             in_=srcv[:, qi * qa:(qi + 1) * qa, :],
             inc=(ldC, ldD)[pbi], incn=16)
        p1_loaded[j] = (pbi, (ldC, ldD)[pbi].n)

    def p1_abs(j):
        m, qi, dr, qa = p1jobs[j]
        pbi, lv = p1_loaded[j]
        q.wait("scalar", (ldC, ldD)[pbi], lv)
        q.op("scalar", "activation", qjunk[:, :], qbuf[pbi][:, :], AF.Abs,
             accum_out=parts[:, m * 4 + qi: m * 4 + qi + 1], inc=sc)
        q.wait("scalar", sc, sc.n)
        q.op("scalar", "copy", dumf[:, 0:1], parts[:, m * 4 + qi:m * 4 + qi + 1], inc=sc)
        qb_g[pbi].release(sc)

    p1_load(0)
    p1_load(1)
    p1_next_abs = [0]
    p1_next_load = [2]
    pg_g = [Guard(), Guard(), Guard()]
    robuf_g = Guard()
    krobuf_g = Guard()
    kss_g = Guard()
    ptrk_g = Guard()

    def emit_rope(src_ps0, src_ps1, dst, th, cs, sn):
        q.op("scalar", "copy", q1s[:, :], src_ps0[:, :])
        q.op("scalar", "copy", q2s[:, :], src_ps1[:, :], inc=sc)
        q.wait("vector", sc, sc.n)
        q.op("vector", "tensor_tensor", rt1[:, :], q1s[:, :], cs, A.mult, inc=vc)
        q.op("vector", "tensor_tensor", rt2[:, :], q2s[:, :], sn, A.mult, inc=vc)
        q.wait("vector", vc, vc.n)
        q.op("vector", "tensor_tensor", dst[:, 0:1024], rt1[:, :], rt2[:, :], A.subtract,
             inc=vc)
        q.wait("vector", vc, vc.n)
        q.op("vector", "tensor_tensor", rt1[:, :], q2s[:, :], cs, A.mult, inc=vc)
        q.op("vector", "tensor_tensor", rt2[:, :], q1s[:, :], sn, A.mult, inc=vc)
        q.wait("vector", vc, vc.n)
        q.op("vector", "tensor_tensor", dst[:, 1024:2048], rt1[:, :], rt2[:, :], A.add,
             inc=vc)

    for h in range(H):
        bi = h % 2
        cstx = (cstA, cstB)[bi]
        if p1_next_abs[0] < 12:
            p1_abs(p1_next_abs[0]); p1_next_abs[0] += 1
            if p1_next_load[0] < 12:
                p1_load(p1_next_load[0]); p1_next_load[0] += 1
        if h >= 4 and p1_next_abs[0] < 12:
            p1_abs(p1_next_abs[0]); p1_next_abs[0] += 1
            if p1_next_load[0] < 12:
                p1_load(p1_next_load[0]); p1_next_load[0] += 1
        wg_g[bi].acquire(q, "gpsimd")
        tv = tbl[bi][:, :].rearrange("p (k t) -> p k t", k=4)
        for ki, tab in enumerate((cqt_d, sqt_d, ckt_d, skt_d)):
            q.op("gpsimd", "dma_start", out=tv[:, ki, :],
                 in_=tab[:, h * T:(h + 1) * T], inc=cstx, incn=16)
        for half in range(2):
            q.op("gpsimd", "dma_start",
                 out=wqp[bi][:, :].rearrange("p (fc c) -> p fc c", fc=16)[:, half * 8:(half + 1) * 8, :],
                 in_=wq_d[:, h * 256:(h + 1) * 256].rearrange("(fc p) c -> p fc c", p=128)[:, half * 8:(half + 1) * 8, :],
                 inc=cstx, incn=16)
            q.op("gpsimd", "dma_start",
                 out=wkp[bi][:, :].rearrange("p (fc c) -> p fc c", fc=16)[:, half * 8:(half + 1) * 8, :],
                 in_=wk_d[:, h * 256:(h + 1) * 256].rearrange("(fc p) c -> p fc c", p=128)[:, half * 8:(half + 1) * 8, :],
                 inc=cstx, incn=16)
        panel_ld = cstx.n
        wqv = wqp[bi][:, :].rearrange("p (fc c) -> p fc c", fc=16)
        wkv = wkp[bi][:, :].rearrange("p (fc c) -> p fc c", fc=16)
        q.wait("tensor", cstx, panel_ld)
        for th in range(2):
            q.wait("tensor", vc, xn_ready[th * 8 + 7])
            for dkc in range(2):
                pg_g[dkc].acquire(q, "tensor")
                for fc in range(FC):
                    for nh in range(2):
                        q.op("tensor", "matmul", pP[dkc][:, nh * 512:(nh + 1) * 512],
                             wqv[:, fc, dkc * 128:(dkc + 1) * 128],
                             xnv[:, fc, th * 1024 + nh * 512: th * 1024 + (nh + 1) * 512],
                             start=(fc == 0), stop=(fc == FC - 1),
                             inc=(tn if (fc == FC - 1 and nh == 1) else None))
            q.wait("scalar", tn, tn.n)
            robuf_g.acquire(q, "scalar")
            emit_rope(pP[0], pP[1], qro, th,
                      tv[:, 0, th * 1024:(th + 1) * 1024],
                      tv[:, 1, th * 1024:(th + 1) * 1024])
            pg_g[0].release(sc)
            pg_g[1].release(sc)
            q.wait("scalar", vc, vc.n)
            qvw = q_d.rearrange("p (ch hd t) -> p ch hd t", ch=NCH, hd=16)
            qrv = qro[:, :].rearrange("p (dkc c4 t) -> p dkc c4 t", dkc=2, c4=4)
            for dkc in range(2):
                q.op("scalar", "dma_start",
                     out=qvw[:, th * 4:(th + 1) * 4, h * 2 + dkc, :],
                     in_=qrv[:, dkc, :, :], inc=stA, incn=16)
            robuf_g.release(stA)
            for dkc in range(2):
                pg_g[dkc].acquire(q, "tensor")
                for fc in range(FC):
                    for nh in range(2):
                        q.op("tensor", "matmul", pP[dkc][:, nh * 512:(nh + 1) * 512],
                             wkv[:, fc, dkc * 128:(dkc + 1) * 128],
                             xnv[:, fc, th * 1024 + nh * 512: th * 1024 + (nh + 1) * 512],
                             start=(fc == 0), stop=(fc == FC - 1),
                             inc=(tn if (fc == FC - 1 and nh == 1) else None))
            q.wait("scalar", tn, tn.n)
            krobuf_g.acquire(q, "scalar")
            robuf_g.acquire(q, "scalar")
            emit_rope(pP[0], pP[1], kro, th,
                      tv[:, 2, th * 1024:(th + 1) * 1024],
                      tv[:, 3, th * 1024:(th + 1) * 1024])
            pg_g[0].release(sc)
            pg_g[1].release(sc)
            rob_v = vc.n
            q.wait("scalar", vc, rob_v)
            kvw = k_d.rearrange("p (ch hd t) -> p ch hd t", ch=NCH, hd=16)
            krv = kro[:, :].rearrange("p (dkc c4 t) -> p dkc c4 t", dkc=2, c4=4)
            for dkc in range(2):
                q.op("scalar", "dma_start",
                     out=kvw[:, th * 4:(th + 1) * 4, h * 2 + dkc, :],
                     in_=krv[:, dkc, :, :], inc=stA, incn=16)
            rob2 = stA.n
            robuf_g.release(stA)
            krov = kro[:, :].rearrange("p (dkc t) -> p dkc t", dkc=2)
            q.wait("tensor", vc, rob_v)
            kss_g.acquire(q, "tensor")
            for tt in range(8):
                ptrk_g.acquire(q, "tensor")
                for dkc in range(2):
                    q.op("tensor", "transpose",
                         ptrK[:, tt % 4 * 256 + dkc * 128: tt % 4 * 256 + (dkc + 1) * 128],
                         krov[:, dkc, tt * 128:(tt + 1) * 128], id_s[:, :],
                         inc=(tn if dkc == 1 else None))
                if tt % 4 == 3 or tt == 7:
                    base = (tt // 4) * 4
                    cnt = tt - base + 1
                    q.wait("vector", tn, tn.n)
                    q.op("vector", "tensor_copy",
                         kss[:, base * 256:(base + cnt) * 256],
                         ptrK[:, 0:cnt * 256], inc=vc)
                    ptrk_g.release(vc)
            krobuf_g.release(vc)
            q.wait("scalar", vc, vc.n)
            q.op("scalar", "dma_start",
                 out=ks_d[th * 1024:(th + 1) * 1024, h * 256:(h + 1) * 256].rearrange(
                     "(tt p) d -> p tt d", p=128),
                 in_=kss[:, :].rearrange("p (tt d) -> p tt d", tt=8), inc=stB, incn=16)
            kss_g.release(stB)
        wg_g[bi].release(tn)
    barrier()
    qk.close()
    a1a.close()

    vgA = ExitStack()
    pg = vgA.enter_context(nc.psum_tensor("pg", [12, 1], F32))
    q.wait("tensor", sc, sc.n)
    q.wait("tensor", vc, vc.n)
    q.op("tensor", "matmul", pg[:, :], parts[:, :], onesc[:, :], start=True, stop=True,
         inc=tn)
    q.wait("vector", tn, tn.n)
    q.op("vector", "tensor_copy", gs3[:, :], pg[:, :], inc=vc)
    q.wait("scalar", vc, vc.n)
    q.op("scalar", "dma_start", out=gin_d.ap(), in_=gs3[:, :], inc=stA, incn=16)
    q.wait("gpsimd", stA, stA.n)
    q.op("gpsimd", "collective_compute", "AllReduce", A.add,
         replica_groups=[list(range(8))], ins=[gin_d.ap()], outs=[gout_d.ap()],
         inc=ccg)
    vgA.close()
    vg = ExitStack()
    wvgp = vg.enter_context(nc.sbuf_tensor("wvgp", [128, 16 * 2048], BF16, side="right"))
    vst = [vg.enter_context(nc.sbuf_tensor(f"vst{i}", [128, 2048], BF16, side="right")) for i in range(2)]
    pV = [vg.enter_context(nc.psum_tensor(f"pV{i}", [128, 2048], F32)) for i in range(2)]
    pv_g = [Guard(), Guard()]
    vst_g = [Guard(), Guard()]
    wvg_g = Guard()
    for cb in range(4):
        src = wv_d if cb < 2 else wg_d
        coff = (cb % 2) * 2048
        wvg_g.acquire(q, "gpsimd")
        for quar in range(4):
            q.op("gpsimd", "dma_start",
                 out=wvgp[:, :].rearrange("p (fc c) -> p fc c", fc=16)[:, quar * 4:(quar + 1) * 4, :],
                 in_=src[:, coff:coff + 2048].rearrange("(fc p) c -> p fc c", p=128)[:, quar * 4:(quar + 1) * 4, :],
                 inc=cstA, incn=16)
        pan_ld = cstA.n
        wvgv = wvgp[:, :].rearrange("p (fc c) -> p fc c", fc=16)
        q.wait("tensor", cstA, pan_ld)
        for t in range(NT):
            bi = t % 2
            pv_g[bi].acquire(q, "tensor")
            for fc in range(FC):
                for nb in range(4):
                    q.op("tensor", "matmul", pV[bi][:, nb * 512:(nb + 1) * 512],
                         xnv[:, fc, t * 128:(t + 1) * 128],
                         wvgv[:, fc, nb * 512:(nb + 1) * 512],
                         start=(fc == 0), stop=(fc == FC - 1),
                         inc=(tn if (fc == FC - 1 and nb == 3) else None))
            tv = tn.n
            if cb < 2:
                q.wait("vector", tn, tv)
                vst_g[bi].acquire(q, "vector")
                q.op("vector", "tensor_copy", vst[bi][:, :], pV[bi][:, :], inc=vc)
                pv_g[bi].release(vc)
                q.wait("scalar", vc, vc.n)
                q.op("scalar", "dma_start",
                     out=v_d[t * 128:(t + 1) * 128, coff:coff + 2048],
                     in_=vst[bi][:, :], inc=(stA, stB)[bi], incn=16)
                vst_g[bi].release((stA, stB)[bi])
            else:
                q.wait("scalar", tn, tv)
                vst_g[bi].acquire(q, "scalar")
                q.op("scalar", "activation", vst[bi][:, :], pV[bi][:, :], AF.Silu, inc=sc)
                pv_g[bi].release(sc)
                q.wait("scalar", sc, sc.n)
                q.op("scalar", "dma_start",
                     out=sg_d[t * 128:(t + 1) * 128, coff:coff + 2048],
                     in_=vst[bi][:, :], inc=(stA, stB)[bi], incn=16)
                vst_g[bi].release((stA, stB)[bi])
        wvg_g.release(tn)
    barrier()
    vg.close()
    xnfm_scope.close()

    sscope = ExitStack()
    Sbuf = [sscope.enter_context(nc.sbuf_tensor(f"Sb{i}", [128, H * 1024], BF16))
            for i in range(2)]

    s1 = ExitStack()
    k1g = [s1.enter_context(nc.sbuf_tensor(f"k1g{i}", [128, 16 * 256], BF16, side="right")) for i in range(2)]
    v1g = [s1.enter_context(nc.sbuf_tensor(f"v1g{i}", [128, 16 * 512], BF16, side="right")) for i in range(2)]
    slb = s1.enter_context(nc.sbuf_tensor("slb", [128, H * 1024], BF16, side="right"))
    pS = [s1.enter_context(nc.psum_tensor(f"pS{i}", [128, 1024], F32)) for i in range(2)]
    pb = s1.enter_context(nc.psum_tensor("pb", [128, 1], F32))
    ps_g = [Guard(), Guard()]
    str_g = [Guard(), Guard()]
    NW = float(HF * D)
    q.wait("gpsimd", ccg, 1)
    q.op("gpsimd", "dma_start", out=gsr[:, :], in_=gout_d.ap().rearrange("a b -> b a"),
         inc=cstA, incn=16)
    q.wait("vector", cstA, cstA.n)
    for m in range(3):
        q.op("vector", "tensor_reduce", g3[:, m:m + 1], gsr[:, 4 * m:4 * m + 4],
             AX.X, A.add, inc=vc)
    q.wait("vector", vc, vc.n)
    q.op("vector", "tensor_scalar", srow[:, :], g3[:, :], 1.0 / NW, 1e-8,
         A.mult, A.add, inc=vc)
    q.wait("vector", vc, vc.n)
    q.op("vector", "reciprocal", srow[:, :], srow[:, :], inc=vc)
    for m in range(3):
        q.wait("tensor", vc, vc.n)
        q.op("tensor", "matmul", pb[:, :], ones1[:, :], srow[0:1, m:m + 1],
             start=True, stop=True, inc=tn)
        q.wait("vector", tn, tn.n)
        q.op("vector", "tensor_copy", scm[:, m:m + 1], pb[:, :], inc=vc)
        q.wait("vector", vc, vc.n)
        q.op("vector", "tensor_scalar_mul", nscm[:, m:m + 1], scm[:, m:m + 1], -1.0,
             inc=vc)
    for h in range(H):
        bi = h % 2
        ldx = (ldA, ldB)[bi]
        str_g[bi].acquire(q, "sync")
        q.op("sync", "dma_start",
             out=k1g[bi][:, :].rearrange("p (tt d) -> p tt d", tt=16),
             in_=ks_d[:, h * 256:(h + 1) * 256].rearrange("(tt p) d -> p tt d", p=128),
             inc=ldx, incn=16)
        q.op("sync", "dma_start",
             out=v1g[bi][:, :].rearrange("p (tt d) -> p tt d", tt=16),
             in_=v_d[:, h * 512:(h + 1) * 512].rearrange("(tt p) d -> p tt d", p=128),
             inc=ldx, incn=16)
        sld = ldx.n
        k1v = k1g[bi][:, :].rearrange("p (tt d) -> p tt d", tt=16)
        v1v = v1g[bi][:, :].rearrange("p (tt d) -> p tt d", tt=16)
        q.wait("tensor", ldx, sld)
        ps_g[bi].acquire(q, "tensor")
        for tt in range(16):
            for dkc in range(2):
                q.op("tensor", "matmul", pS[bi][:, dkc * 512:(dkc + 1) * 512],
                     k1v[:, tt, dkc * 128:(dkc + 1) * 128],
                     v1v[:, tt, :],
                     start=(tt == 0), stop=(tt == 15),
                     inc=(tn if (tt == 15 and dkc == 1) else None))
        q.wait("vector", tn, tn.n)
        q.op("vector", "tensor_copy", slb[:, h * 1024:(h + 1) * 1024], pS[bi][:, :],
             inc=vc)
        ps_g[bi].release(vc)
        str_g[bi].release(tn)
    q.wait("scalar", vc, vc.n)
    q.op("scalar", "dma_start", out=si_d.ap(), in_=slb[:, :], inc=stA, incn=16)
    q.wait("gpsimd", stA, stA.n)
    q.op("gpsimd", "collective_compute", "AllGather", A.bypass,
         replica_groups=[[0, 1], [2, 3], [4, 5], [6, 7]],
         ins=[si_d.ap()], outs=[so_d.ap()], inc=ccs)
    s1.close()

    p0 = ExitStack()
    shard2 = [p0.enter_context(nc.sbuf_tensor(f"shard{i}", [128, 11264], F32, side="right"))
              for i in range(2)]
    bufA = p0.enter_context(nc.sbuf_tensor("bufA", [128, 11264], BF16, side="right"))
    bufB = p0.enter_context(nc.sbuf_tensor("bufB", [128, 11264], BF16, side="right"))
    bufC = p0.enter_context(nc.sbuf_tensor("bufC", [128, 11264], BF16, side="right"))
    ptr0 = p0.enter_context(nc.psum_tensor("ptr0", [128, 1024], BF16))

    shard_g = [Guard(), Guard()]
    shard_ld = [None, None, None]

    def p2_load(m):
        dr2 = mats[m][0]
        va2 = mats[m][1][0]
        mbi = m % 2
        ldw2 = (ldC, ldD)[mbi]
        srcv = dr2.rearrange("(a p) f -> p a f", p=128)
        half2 = va2 // 2
        shard_g[mbi].acquire(q, "sync")
        q.op("sync", "dma_start",
             out=shard2[mbi][:, :].rearrange("p (a f) -> p a f", a=va2)[:, 0:half2, :],
             in_=srcv[:, 0:half2, :], inc=ldw2, incn=16)
        q.op("sync", "dma_start",
             out=shard2[mbi][:, :].rearrange("p (a f) -> p a f", a=va2)[:, half2:va2, :],
             in_=srcv[:, half2:va2, :], inc=ldw2, incn=16)
        shard_ld[m] = (mbi, ldw2.n)

    q.wait("sync", tn, tn.n)
    q.wait("sync", vc, vc.n)
    q.wait("sync", stA, stA.n)
    p2_load(0)
    p2_load(1)
    for m, (dr, (va, vb), gi, go, ccx, kind) in enumerate(mats):
        if m + 2 < 3:
            p2_load(m + 2)
        mbi, mlv = shard_ld[m]
        shard = shard2[mbi]
        q.wait("vector", (ldC, ldD)[mbi], mlv)
        q.wait("vector", tn, tn.n)
        q.wait("vector", stA, stA.n)
        q.op("vector", "tensor_scalar", bufA[:, :], shard[:, :], scm[:, m:m + 1], 0.5,
             A.mult, A.is_ge, inc=vc)
        q.op("vector", "tensor_scalar", bufB[:, :], shard[:, :], nscm[:, m:m + 1], 0.5,
             A.mult, A.is_ge, inc=vc)
        shard_g[mbi].release(vc)
        q.wait("vector", vc, vc.n)
        q.op("vector", "tensor_tensor", bufC[:, :], bufA[:, :], bufB[:, :], A.subtract, inc=vc)
        wv_view = bufC[:, :].rearrange("p (a f) -> p a f", a=va)
        if kind == "w12":
            stg = bufA[:, :].rearrange("p (fc h) -> p fc h", fc=2)
            q.wait("tensor", vc, vc.n)
            for fc in range(2):
                for g0 in range(0, 44, 8):
                    gn = min(8, 44 - g0)
                    for j in range(gn):
                        q.op("tensor", "transpose", ptr0[:, j * 128:(j + 1) * 128],
                             wv_view[:, g0 + j, fc * 128:(fc + 1) * 128], id_s[:, :],
                             inc=(tn if j == gn - 1 else None))
                    q.wait("vector", tn, tn.n)
                    q.op("vector", "tensor_copy", stg[:, fc, g0 * 128:(g0 + gn) * 128],
                         ptr0[:, 0:gn * 128], inc=vc)
                    q.wait("tensor", vc, vc.n)
            q.wait("scalar", vc, vc.n)
            q.op("scalar", "dma_start", out=gi.ap().rearrange("(fc p) h -> p fc h", p=128),
                 in_=stg[:, :, :], inc=stA, incn=16)
        else:
            stg3_g = Guard()
            for hc in range(6):
                w = 128 if hc < 5 else 64
                stghc = bufA[:, 0:2048]
                q.wait("tensor", vc, vc.n)
                stg3_g.acquire(q, "vector")
                for g0 in range(0, 16, 8):
                    for j in range(8):
                        dd = g0 + j
                        q.op("tensor", "transpose", ptr0[0:w, j * 128:(j + 1) * 128],
                             wv_view[:, dd, hc * 128:hc * 128 + w], id_s[:, :],
                             inc=(tn if j == 7 else None))
                    q.wait("vector", tn, tn.n)
                    q.op("vector", "tensor_copy", stghc[0:w, g0 * 128:(g0 + 8) * 128],
                         ptr0[0:w, 0:1024], inc=vc)
                    q.wait("tensor", vc, vc.n)
                q.wait("scalar", vc, vc.n)
                q.op("scalar", "dma_start", out=gi.ap()[hc * 128:hc * 128 + w, :],
                     in_=stghc[0:w, :], inc=stA, incn=16)
                stg3_g.release(stA)
        if m == 0 and "w1q_raw" in debug_out:
            q.wait("scalar", vc, vc.n)
            q.op("scalar", "dma_start", out=w1q_raw_d.ap(), in_=bufC[:, :], inc=stA, incn=16)
            q.op("scalar", "dma_start", out=qsc_d.ap()[:, 0:3], in_=scm[:, :], inc=stA, incn=16, allow_slow_non_contiguous=True)
            q.op("scalar", "dma_start", out=qsc_d.ap()[:, 3:6], in_=nscm[:, :], inc=stA, incn=16, allow_slow_non_contiguous=True)
            q.wait("gpsimd", stA, stA.n)
            q.op("gpsimd", "dma_start", out=w1i_dbg_d.ap(), in_=gi.ap(), inc=cstA, incn=16)
        q.wait("gpsimd", stA, stA.n)
        q.op("gpsimd", "collective_compute", "AllGather", A.bypass,
             replica_groups=[list(range(8))], ins=[gi.ap()], outs=[go.ap()],
             inc=ccx)
    if "w1qt_dbg" in debug_out:
        q.wait("gpsimd", ccw1, 1)
        q.wait("gpsimd", ccw3, 1)
        q.op("gpsimd", "dma_start", out=w1dbg_d.ap(), in_=w1t_d.ap(), inc=cstA, incn=16)
        q.op("gpsimd", "dma_start", out=w3dbg_d.ap(), in_=w3t_d.ap(), inc=cstA, incn=16)
    sinscope = ExitStack()
    sin_b = sinscope.enter_context(nc.sbuf_tensor("sin_b", [128, H * 1024], BF16))
    q.wait("gpsimd", ccs, 1)
    q.op("gpsimd", "dma_start", out=sin_b[:, :], in_=so_d.ap()[0:128, :], inc=cstB, incn=16)
    q.wait("vector", cstB, cstB.n)
    for h in range(H):
        q.op("vector", "tensor_scalar_mul", Sbuf[0][:, h * 1024:(h + 1) * 1024],
             sin_b[:, h * 1024:(h + 1) * 1024], scol_s[:, h:h + 1], inc=vc)
    barrier()
    p0.close()
    sinscope.close()

    a2 = ExitStack()
    qsb = [a2.enter_context(nc.sbuf_tensor(f"qsb{i}", [128, 16 * 256], BF16, side="right")) for i in range(2)]
    ksb = [a2.enter_context(nc.sbuf_tensor(f"ksb{i}", [128, 16 * 256], BF16, side="right")) for i in range(2)]
    k2b = [a2.enter_context(nc.sbuf_tensor(f"k2b{i}", [128, 2 * 2048], BF16, side="right")) for i in range(2)]
    v2b = [a2.enter_context(nc.sbuf_tensor(f"v2b{i}", [128, 2 * 4096], BF16, side="right")) for i in range(2)]
    g2b = [a2.enter_context(nc.sbuf_tensor(f"g2b{i}", [128, 2 * 4096], BF16, side="right")) for i in range(2)]
    stsb = [a2.enter_context(nc.sbuf_tensor(f"stsb{i}", [128, 384], BF16, side="right"))
            for i in range(2)]
    oall = [a2.enter_context(nc.sbuf_tensor(f"oall{i}", [128, 16 * 512], BF16, side="right")) for i in range(2)]
    junk2 = a2.enter_context(nc.sbuf_tensor("junk2", [128, 512], BF16, side="right"))
    rsums = a2.enter_context(nc.sbuf_tensor("rsums", [128, 32], F32, side="right"))
    rmsc = a2.enter_context(nc.sbuf_tensor("rmsc", [128, 32], F32, side="right"))
    ost = a2.enter_context(nc.sbuf_tensor("ost", [128, 1024], BF16, side="right"))
    psA = [a2.enter_context(nc.psum_tensor(f"psA{i}", [128, 512], F32)) for i in range(2)]
    pOC = [a2.enter_context(nc.psum_tensor(f"pOC{i}", [128, 1024], F32)) for i in range(2)]
    pST = a2.enter_context(nc.psum_tensor("pST", [128, 512], F32))
    ptr2 = a2.enter_context(nc.psum_tensor("ptr2", [128, 1024], BF16))

    buf_g = [Guard(), Guard()]
    pST_g, ptr2_g = Guard(), Guard()
    psA_gv = [Guard(), Guard()]
    poc_g = [Guard(), Guard()]
    stsb_g, ost_g = Guard(), Guard()

    def load_chunk(ch):
        bi = ch % 2
        ldx = (ldA, ldB)[bi]
        buf_g[bi].acquire(q, "sync")
        c0 = ch * 256
        q.op("sync", "dma_start",
             out=qsb[bi][:, :],
             in_=q_d[:, ch * 4096:(ch + 1) * 4096],
             inc=ldx, incn=16)
        q.op("sync", "dma_start",
             out=ksb[bi][:, :],
             in_=k_d[:, ch * 4096:(ch + 1) * 4096],
             inc=ldx, incn=16)
        q.op("sync", "dma_start",
             out=k2b[bi][:, :].rearrange("p (tt d) -> p tt d", tt=2),
             in_=ks_d[c0:c0 + 256, :].rearrange("(tt p) d -> p tt d", p=128),
             inc=ldx, incn=16)
        q.op("sync", "dma_start",
             out=v2b[bi][:, :].rearrange("p (tt d) -> p tt d", tt=2),
             in_=v_d[c0:c0 + 256, :].rearrange("(tt p) d -> p tt d", p=128),
             inc=ldx, incn=16)
        q.op("sync", "dma_start",
             out=g2b[bi][:, :].rearrange("p (tt d) -> p tt d", tt=2),
             in_=sg_d[c0:c0 + 256, :].rearrange("(tt p) d -> p tt d", p=128),
             inc=ldx, incn=16)
        return (bi, ldx.n)

    chunk_ld = [None] * NCH
    chunk_ld[0] = load_chunk(0)

    pend = {"tail": None, "tr": []}

    def make_tail(ch, bi, sq_done, g2v):
        def emit():
            ro = bi * 16
            q.wait("vector", sc, sq_done)
            q.op("vector", "tensor_scalar", rmsc[:, ro:ro + 16], rsums[:, ro:ro + 16],
                 1.0 / DV, 1e-5, A.mult, A.add, inc=vc)
            q.wait("scalar", vc, vc.n)
            q.op("scalar", "activation", rmsc[:, ro:ro + 16], rmsc[:, ro:ro + 16],
                 AF.Ln, inc=sc)
            q.wait("scalar", sc, sc.n)
            q.op("scalar", "activation", rmsc[:, ro:ro + 16], rmsc[:, ro:ro + 16],
                 AF.Exp, scale=-0.5, inc=sc)
            q.wait("vector", sc, sc.n)
            for hh in range(H):
                for tp in range(2):
                    sl = oall[bi][:, (hh * 2 + tp) * 512:(hh * 2 + tp + 1) * 512]
                    q.op("vector", "scalar_tensor_tensor", sl, sl,
                         rmsc[:, ro + hh * 2 + tp: ro + hh * 2 + tp + 1],
                         g2v[:, tp, hh * 512:(hh + 1) * 512], A.mult, A.mult, inc=vc)
            return vc.n
        return emit

    def make_tr(ch, bi, hh):
        def emit(gate_v):
            q.wait("tensor", vc, gate_v)
            ptr2_g.acquire(q, "tensor")
            for dvc in range(4):
                for tp in range(2):
                    q.op("tensor", "transpose",
                         ptr2[:, dvc * 256 + tp * 128: dvc * 256 + (tp + 1) * 128],
                         oall[bi][:, (hh * 2 + tp) * 512 + dvc * 128:
                                  (hh * 2 + tp) * 512 + (dvc + 1) * 128],
                         id_s[:, :],
                         inc=(tn if (dvc == 3 and tp == 1) else None))
            q.wait("vector", tn, tn.n)
            ost_g.acquire(q, "vector")
            q.op("vector", "tensor_copy", ost[:, :], ptr2[:, :], inc=vc)
            ptr2_g.release(vc)
            q.wait("scalar", vc, vc.n)
            q.op("scalar", "dma_start",
                 out=ot_d[:, ch * 8192 + hh * 1024: ch * 8192 + (hh + 1) * 1024],
                 in_=ost[:, :],
                 inc=stA, incn=16)
            ost_g.release(stA)
        return emit

    def flush_pending():
        gate_v = None
        if pend["tail"] is not None:
            gate_v = pend["tail"]()
            pend["tail"] = None
        trs = pend["tr"]
        pend["tr"] = []
        return gate_v, trs

    for ch in range(NCH):
        bi = ch % 2
        Sp = Sbuf[ch % 2]
        Sn = Sbuf[(ch + 1) % 2]
        gate_v, trs = flush_pending()
        if gate_v is not None:
            buf_g[(ch - 1) % 2].sem, buf_g[(ch - 1) % 2].val = vc, gate_v
        if ch + 1 < NCH:
            chunk_ld[ch + 1] = load_chunk(ch + 1)
        qv = qsb[bi][:, :].rearrange("p (hd t) -> p hd t", hd=16)
        kv = ksb[bi][:, :].rearrange("p (hd t) -> p hd t", hd=16)
        k2v = k2b[bi][:, :].rearrange("p (tt d) -> p tt d", tt=2)
        v2v = v2b[bi][:, :].rearrange("p (tt d) -> p tt d", tt=2)
        g2v = g2b[bi][:, :].rearrange("p (tt d) -> p tt d", tt=2)
        q.wait("tensor", (ldA, ldB)[chunk_ld[ch][0]], chunk_ld[ch][1])
        prev_o = [None]
        for h in range(H):
            hs = h % 2
            psA_gv[hs].acquire(q, "tensor")
            for dkc in range(2):
                q.op("tensor", "matmul", psA[hs][:, 0:256],
                     kv[:, h * 2 + dkc, 0:128], qv[:, h * 2 + dkc, 0:256],
                     start=(dkc == 0), stop=(dkc == 1))
            for dkc in range(2):
                q.op("tensor", "matmul", psA[hs][:, 256:384],
                     kv[:, h * 2 + dkc, 128:256], qv[:, h * 2 + dkc, 128:256],
                     start=(dkc == 0), stop=(dkc == 1),
                     inc=(tn if dkc == 1 else None))
            sc_mm = tn.n
            q.wait("vector", tn, sc_mm)
            stsb_g.acquire(q, "vector")
            q.op("vector", "tensor_tensor", stsb[hs][:, 0:384], psA[hs][:, 0:384],
                 cmask_s[:, 0:384], A.mult, inc=vc)
            evv = vc.n
            psA_gv[hs].release(vc)
            pST_g.acquire(q, "tensor")
            for tt in range(2):
                q.op("tensor", "matmul", pST[:, :],
                     k2v[:, tt, h * 256: h * 256 + 128],
                     v2v[:, tt, h * 512:(h + 1) * 512],
                     start=(tt == 0), stop=(tt == 1),
                     inc=(tn if tt == 1 else None))
            st0 = tn.n
            if prev_o[0] is not None:
                prev_o[0]()
                prev_o[0] = None
            if trs:
                trs.pop(0)(gate_v)
            q.wait("vector", tn, st0)
            q.op("vector", "tensor_tensor", Sn[:, h * 1024:h * 1024 + 512],
                 Sp[:, h * 1024:h * 1024 + 512], pST[:, :], A.add, inc=vc)
            ev0 = vc.n
            pST_g.release(vc)
            pST_g.acquire(q, "tensor")
            for tt in range(2):
                q.op("tensor", "matmul", pST[:, :],
                     k2v[:, tt, h * 256 + 128: h * 256 + 256],
                     v2v[:, tt, h * 512:(h + 1) * 512],
                     start=(tt == 0), stop=(tt == 1),
                     inc=(tn if tt == 1 else None))
            st1 = tn.n
            q.wait("vector", tn, st1)
            q.op("vector", "tensor_tensor", Sn[:, h * 1024 + 512:(h + 1) * 1024],
                 Sp[:, h * 1024 + 512:(h + 1) * 1024], pST[:, :], A.add, inc=vc)
            pST_g.release(vc)

            def make_o(h=h, hs=hs, evv=evv, bi=bi, ch=ch, Sp=Sp,
                       qv=qv, v2v=v2v):
                def emit():
                    poc_g[hs].acquire(q, "tensor")
                    q.wait("tensor", vc, evv)
                    q.op("tensor", "matmul", pOC[hs][:, 0:512], stsb[hs][:, 0:128],
                         v2v[:, 0, h * 512:(h + 1) * 512], start=True, stop=False)
                    for dkc in range(2):
                        q.op("tensor", "matmul", pOC[hs][:, 0:512],
                             qv[:, h * 2 + dkc, 0:128],
                             Sp[:, (h * 2 + dkc) * 512:(h * 2 + dkc + 1) * 512],
                             start=False, stop=(dkc == 1))
                    q.op("tensor", "matmul", pOC[hs][:, 512:1024], stsb[hs][:, 128:256],
                         v2v[:, 0, h * 512:(h + 1) * 512], start=True, stop=False)
                    q.op("tensor", "matmul", pOC[hs][:, 512:1024], stsb[hs][:, 256:384],
                         v2v[:, 1, h * 512:(h + 1) * 512], start=False, stop=False)
                    for dkc in range(2):
                        q.op("tensor", "matmul", pOC[hs][:, 512:1024],
                             qv[:, h * 2 + dkc, 128:256],
                             Sp[:, (h * 2 + dkc) * 512:(h * 2 + dkc + 1) * 512],
                             start=False, stop=(dkc == 1),
                             inc=(tn if dkc == 1 else None))
                    omm = tn.n
                    q.wait("scalar", tn, omm)
                    q.op("scalar", "copy",
                         oall[bi][:, h * 1024:(h + 1) * 1024], pOC[hs][:, :], inc=sc)
                    poc_g[hs].release(sc)
                    q.wait("scalar", sc, sc.n)
                    for tp in range(2):
                        q.op("scalar", "activation", junk2[:, :],
                             oall[bi][:, (h * 2 + tp) * 512:(h * 2 + tp + 1) * 512],
                             AF.Square,
                             accum_out=rsums[:, bi * 16 + h * 2 + tp:
                                             bi * 16 + h * 2 + tp + 1], inc=sc)
                        q.wait("scalar", sc, sc.n)
                    q.op("scalar", "copy", dumf[:, 0:2],
                         rsums[:, bi * 16 + h * 2: bi * 16 + h * 2 + 2], inc=sc)
                return emit
            prev_o[0] = make_o()
        if prev_o[0] is not None:
            prev_o[0]()
            prev_o[0] = None
        for tr in trs:
            tr(gate_v)
        sq_done = sc.n
        pend["tail"] = make_tail(ch, bi, sq_done, g2v)
        pend["tr"] = [make_tr(ch, bi, hh) for hh in range(H)]
        buf_g[bi].sem, buf_g[bi].val = vc, vc.n
    gate_v, trs = flush_pending()
    for tr in trs:
        tr(gate_v)
    barrier()
    a2.close()
    sscope.close()
    gscope.close()
    rom2.close()

    a3 = ExitStack()
    wo_b = a3.enter_context(nc.sbuf_tensor("wo_b", [128, 32 * 2048], BF16, side="right"))
    otp = [a3.enter_context(nc.sbuf_tensor(f"otp{i}", [128, 32 * 256], BF16, side="right")) for i in range(2)]
    atb = [a3.enter_context(nc.sbuf_tensor(f"atb{i}", [128, 2048], BF16, side="right")) for i in range(2)]
    pW = [a3.enter_context(nc.psum_tensor(f"pW{i}", [128, 2048], F32)) for i in range(2)]
    wov = wo_b[:, :].rearrange("p (oc d) -> p oc d", oc=32)
    for g8 in range(8):
        q.op("gpsimd", "dma_start",
             out=wov[:, g8 * 4:(g8 + 1) * 4, :],
             in_=wo_d.rearrange("(oc p) d -> p oc d", p=128)[:, g8 * 4:(g8 + 1) * 4, :],
             inc=cstA, incn=16)
    wo_ld = cstA.n
    q.wait("vector", cstA, wo_ld)
    for oc in range(32):
        q.op("vector", "tensor_scalar_mul", wov[:, oc, :], wov[:, oc, :],
             gnc_s[:, oc % 4: oc % 4 + 1], inc=(vc if oc == 31 else None))
    wo_scaled = vc.n
    otp_g = [Guard(), Guard()]
    pw_g = [Guard(), Guard()]
    atb_g = [Guard(), Guard()]
    q.wait("tensor", vc, wo_scaled)
    for tpair in range(8):
        bi = tpair % 2
        ldx = (ldA, ldB)[bi]
        otp_g[bi].acquire(q, "sync")
        q.op("sync", "dma_start",
             out=otp[bi][:, :],
             in_=ot_d[:, tpair * 8192:(tpair + 1) * 8192],
             inc=ldx, incn=16)
        ot_ld = ldx.n
        otv = otp[bi][:, :].rearrange("p (ci ov t) -> p ci ov t", ci=1, ov=32)
        q.wait("tensor", ldx, ot_ld)
        for ti in range(2):
            tau = tpair * 2 + ti
            pw_g[ti].acquire(q, "tensor")
            for ov in range(32):
                for nb in range(4):
                    q.op("tensor", "matmul", pW[ti][:, nb * 512:(nb + 1) * 512],
                         otv[:, 0, ov, ti * 128:(ti + 1) * 128],
                         wov[:, ov, nb * 512:(nb + 1) * 512],
                         start=(ov == 0), stop=(ov == 31),
                         inc=(tn if (ov == 31 and nb == 3) else None))
            wmm = tn.n
            q.wait("vector", tn, wmm)
            atb_g[ti].acquire(q, "vector")
            q.op("vector", "tensor_copy", atb[ti][:, :], pW[ti][:, :], inc=vc)
            pw_g[ti].release(vc)
            q.wait("scalar", vc, vc.n)
            q.op("scalar", "dma_start", out=at_d[tau * 128:(tau + 1) * 128, :],
                 in_=atb[ti][:, :], inc=(stA, stB)[ti], incn=16)
            atb_g[ti].release((stA, stB)[ti])
            if ti == 1:
                otp_g[bi].release(tn)
    barrier()
    a3.close()

    bscope = ExitStack()
    hid = bscope.enter_context(nc.sbuf_tensor("hid", [128, 44 * 1024], BF16, side="right"))
    w3p0 = bscope.enter_context(nc.sbuf_tensor("w3p0", [128, 44 * 512], BF16, side="right"))
    w3_g = [Guard(), Guard()]
    hv = hid[:, :].rearrange("p (ht t) -> p ht t", ht=44)
    q.wait("sync", ccw1, 1)
    q.wait("sync", ccw2, 1)
    q.wait("sync", ccw3, 1)
    for ps in range(2):
        w3_ld2 = [0, 0]
        b1 = ExitStack()
        xnp = b1.enter_context(nc.sbuf_tensor(f"xnp{ps}", [128, 16 * 1024], BF16, side="right"))
        w1p = [b1.enter_context(nc.sbuf_tensor(f"w1p{ps}_{i}", [128, 16 * 256], BF16, side="right")) for i in range(2)]
        w2p = [b1.enter_context(nc.sbuf_tensor(f"w2p{ps}_{i}", [128, 16 * 256], BF16, side="right")) for i in range(2)]
        sgt = b1.enter_context(nc.sbuf_tensor(f"sgt{ps}", [128, 1024], BF16, side="right"))
        pX = [b1.enter_context(nc.psum_tensor(f"pX{ps}_{i}", [128, 1024], F32)) for i in range(4)]
        xnpv = xnp[:, :].rearrange("p (fc t) -> p fc t", fc=16)
        w3v0 = w3p0[:, :].rearrange("p (ht d) -> p ht d", ht=44)
        w3_g[0].acquire(q, "sync")
        for quar in range(4):
            q.op("sync", "dma_start", out=w3v0[:, quar * 11:(quar + 1) * 11, :],
                 in_=w3t_d.ap()[:, 0:512].rearrange(
                     "(ht p) d -> p ht d", p=128)[:, quar * 11:(quar + 1) * 11, :],
                 inc=ldD, incn=16)
        w3_ld2[0] = (ldD, ldD.n)
        for fc in range(FC):
            q.op("sync", "dma_start", out=xnpv[:, fc, :],
                 in_=xn_d[fc * 128:(fc + 1) * 128, ps * 1024:(ps + 1) * 1024],
                 inc=ldC, incn=16)
        xn_ld = ldC.n
        wp_g = [Guard(), Guard()]
        px_g = [Guard(), Guard(), Guard(), Guard()]
        sgt_g = Guard()
        hid_g = Guard()
        hid_g.acquire(q, "vector")
        for htp in range(22):
            bi = htp % 2
            ldx = (ldA, ldB)[bi]
            wp_g[bi].acquire(q, "sync")
            q.op("sync", "dma_start",
                 out=w1p[bi][:, :].rearrange("p (fc h) -> p fc h", fc=16),
                 in_=w1t_d.ap()[:, htp * 256:(htp + 1) * 256].rearrange("(fc p) h -> p fc h", p=128),
                 inc=ldx, incn=16)
            q.op("sync", "dma_start",
                 out=w2p[bi][:, :].rearrange("p (fc h) -> p fc h", fc=16),
                 in_=w2t_d.ap()[:, htp * 256:(htp + 1) * 256].rearrange("(fc p) h -> p fc h", p=128),
                 inc=ldx, incn=16)
            w_ld = ldx.n
            w1v = w1p[bi][:, :].rearrange("p (fc h) -> p fc h", fc=16)
            w2v = w2p[bi][:, :].rearrange("p (fc h) -> p fc h", fc=16)
            q.wait("tensor", ldx, w_ld)
            q.wait("tensor", ldC, xn_ld)
            for hh in range(2):
                ht = htp * 2 + hh
                p1i, p2i = (2 * bi), (2 * bi + 1)
                px_g[p1i].acquire(q, "tensor")
                px_g[p2i].acquire(q, "tensor")
                for fc in range(FC):
                    for nh in range(2):
                        q.op("tensor", "matmul", pX[p1i][:, nh * 512:(nh + 1) * 512],
                             w1v[:, fc, hh * 128:(hh + 1) * 128],
                             xnpv[:, fc, nh * 512:(nh + 1) * 512],
                             start=(fc == 0), stop=(fc == FC - 1))
                for fc in range(FC):
                    for nh in range(2):
                        q.op("tensor", "matmul", pX[p2i][:, nh * 512:(nh + 1) * 512],
                             w2v[:, fc, hh * 128:(hh + 1) * 128],
                             xnpv[:, fc, nh * 512:(nh + 1) * 512],
                             start=(fc == 0), stop=(fc == FC - 1),
                             inc=(tn if (fc == FC - 1 and nh == 1) else None))
                x_mm = tn.n
                q.wait("scalar", tn, x_mm)
                sgt_g.acquire(q, "scalar")
                q.op("scalar", "activation", sgt[:, :], pX[p1i][:, :], AF.Sigmoid, inc=sc)
                px_g[p1i].release(sc)
                q.wait("vector", sc, sc.n)
                q.op("vector", "tensor_tensor", hv[:, ht, :], sgt[:, :], pX[p2i][:, :],
                     A.mult, inc=vc)
                px_g[p2i].release(vc)
                sgt_g.release(vc)
            wp_g[bi].release(tn)
        b1_done_v = vc.n
        b1.close()
        q.wait("sync", tn, tn.n)
        q.wait("sync", vc, vc.n)
        q.wait("sync", sc, sc.n)
        b3 = ExitStack()
        w3p1 = b3.enter_context(nc.sbuf_tensor(f"w3p1_{ps}", [128, 44 * 512], BF16, side="right"))
        w3p = [w3p0, w3p1]
        xtb = [b3.enter_context(nc.sbuf_tensor(f"xtb{ps}_{i}", [128, 512], F32, side="right")) for i in range(2)]
        antb = [b3.enter_context(nc.sbuf_tensor(f"antb{ps}_{i}", [128, 512], BF16, side="right")) for i in range(2)]
        ob = [b3.enter_context(nc.sbuf_tensor(f"ob{ps}_{i}", [128, 512], F32, side="right")) for i in range(2)]
        pF = [b3.enter_context(nc.psum_tensor(f"pF{ps}_{i}", [128, 512], F32)) for i in range(2)]
        pf_g = [Guard(), Guard()]
        xt_g = [Guard(), Guard()]
        ob_g = [Guard(), Guard()]
        for dq in range(4):
            wbi = dq % 2
            ldw = (ldC, ldD)[wbi]
            w3v = w3p[wbi][:, :].rearrange("p (ht d) -> p ht d", ht=44)
            if dq >= 1:
                w3_g[wbi].acquire(q, "sync")
                for quar in range(4):
                    q.op("sync", "dma_start", out=w3v[:, quar * 11:(quar + 1) * 11, :],
                         in_=w3t_d.ap()[:, dq * 512:(dq + 1) * 512].rearrange(
                             "(ht p) d -> p ht d", p=128)[:, quar * 11:(quar + 1) * 11, :],
                         inc=ldw, incn=16)
                w3_ld = ldw.n
            else:
                ldw, w3_ld = w3_ld2[dq]
            for t in range(8):
                bi = t % 2
                tau = ps * 8 + t
                ldx = (ldA, ldB)[bi]
                xt_g[bi].acquire(q, "sync")
                q.op("sync", "dma_start", out=xtb[bi][:, :],
                     in_=x_d[tau * 128:(tau + 1) * 128, dq * 512:(dq + 1) * 512],
                     inc=ldx, incn=16)
                q.op("sync", "dma_start", out=antb[bi][:, :],
                     in_=at_d[tau * 128:(tau + 1) * 128, dq * 512:(dq + 1) * 512],
                     inc=ldx, incn=16)
                tile_ld = ldx.n
                pf_g[bi].acquire(q, "tensor")
                q.wait("tensor", ldw, w3_ld)
                q.wait("tensor", vc, b1_done_v)
                for ht in range(44):
                    q.op("tensor", "matmul", pF[bi][:, :], hv[:, ht, t * 128:(t + 1) * 128],
                         w3v[:, ht, :], start=(ht == 0), stop=(ht == 43),
                         inc=(tn if ht == 43 else None))
                f_mm = tn.n
                q.wait("vector", tn, f_mm)
                q.wait("vector", ldx, tile_ld)
                ob_g[bi].acquire(q, "vector")

                q.op("vector", "tensor_tensor", ob[bi][:, :], pF[bi][:, :], xtb[bi][:, :],
                     A.add, inc=vc)
                q.wait("vector", vc, vc.n)
                q.op("vector", "tensor_tensor", ob[bi][:, :], ob[bi][:, :], antb[bi][:, :],
                     A.add, inc=vc)
                pf_g[bi].release(vc)
                xt_g[bi].release(vc)
                q.wait("scalar", vc, vc.n)
                q.op("scalar", "dma_start",
                     out=y_d[tau * 128:(tau + 1) * 128, dq * 512:(dq + 1) * 512],
                     in_=ob[bi][:, :], inc=(stA, stB)[bi], incn=16)
                ob_g[bi].release((stA, stB)[bi])
            w3_g[wbi].release(tn)
        b3.close()
        if ps == 0:
            barrier()
    bscope.close()

    q.wait("scalar", stA, stA.n)
    q.wait("scalar", stB, stB.n)
    q.wait("sync", ldA, ldA.n)
    q.wait("sync", ldB, ldB.n)
    q.wait("sync", ldC, ldC.n)
    q.wait("gpsimd", cstA, cstA.n)
    q.wait("gpsimd", cstB, cstB.n)

    block = E(nc.Block())

    @block.sync
    def _(e):
        for f in q.p["sync"]:
            f(e)

    @block.gpsimd
    def _(e):
        for f in q.p["gpsimd"]:
            f(e)

    @block.tensor
    def _(e):
        for f in q.p["tensor"]:
            f(e)

    @block.vector
    def _(e):
        for f in q.p["vector"]:
            f(e)

    @block.scalar
    def _(e):
        for f in q.p["scalar"]:
            f(e)

    es.close()
    return nc


def _host_constants(half):
    g = _gammas()
    t0 = half * T
    tl = np.arange(T, dtype=np.float64)
    tabs = tl + t0
    invf = 10000.0 ** (-np.arange(0, DK, 2, dtype=np.float64) / DK)
    ang = tabs[None, :] * invf[:, None]
    c = np.cos(ang)
    s = np.sin(ang)
    cqt = np.zeros((128, H * T), np.float64)
    sqt = np.zeros((128, H * T), np.float64)
    ckt = np.zeros((128, H * T), np.float64)
    skt = np.zeros((128, H * T), np.float64)
    for h in range(H):
        lg = np.log(g[h])
        gq = np.exp(lg * (tl + 1.0)) * SCALE
        gk = np.exp(-lg * (tl + 1.0))
        cqt[:, h * T:(h + 1) * T] = c * gq[None, :]
        sqt[:, h * T:(h + 1) * T] = s * gq[None, :]
        ckt[:, h * T:(h + 1) * T] = c * gk[None, :]
        skt[:, h * T:(h + 1) * T] = s * gk[None, :]
    cmask = np.zeros((128, 384), np.float32)
    jj = np.arange(128)
    tril = (jj[None, :] >= jj[:, None]).astype(np.float32)
    cmask[:, 0:128] = tril
    cmask[:, 128:256] = 1.0
    cmask[:, 256:384] = tril
    scol = np.zeros((128, H), np.float32)
    if half == 1:
        for h in range(H):
            scol[:, h] = np.exp(np.log(g[h]) * 2048.0)
    ident = np.eye(128, dtype=ml_dtypes.bfloat16)
    bf = ml_dtypes.bfloat16
    return (cqt.astype(bf), sqt.astype(bf), ckt.astype(bf), skt.astype(bf),
            cmask, scol, ident)


_NC_CACHE = {}


def kernel(L0_attn_hidden_states, L0_attn_Wq, L0_attn_Wk, L0_attn_Wv, L0_attn_Wg,
           L0_attn_Wo, L0_attn_g_norm_weight, L0_ffn_w1, L0_ffn_w2, L0_ffn_w3):
    x = np.asarray(L0_attn_hidden_states, np.float32)
    wq = np.ascontiguousarray(np.asarray(L0_attn_Wq, np.float32))
    wk = np.ascontiguousarray(np.asarray(L0_attn_Wk, np.float32))
    wv = np.ascontiguousarray(np.asarray(L0_attn_Wv, np.float32))
    wg = np.ascontiguousarray(np.asarray(L0_attn_Wg, np.float32))
    wo = np.ascontiguousarray(np.asarray(L0_attn_Wo, np.float32))
    w1 = np.asarray(L0_ffn_w1, np.float32)
    w2 = np.asarray(L0_ffn_w2, np.float32)
    w3 = np.asarray(L0_ffn_w3, np.float32)
    gn = np.asarray(L0_attn_g_norm_weight, np.float32)
    gnc = np.ascontiguousarray(gn.reshape(4, 128).T)

    if "nc" not in _NC_CACHE:
        _NC_CACHE["nc"] = build_program()
    nc = _NC_CACHE["nc"]

    consts = [_host_constants(0), _host_constants(1)]
    in_maps = []
    for c in range(8):
        b, half = c // 2, c % 2
        cqt, sqt, ckt, skt, cmask, scol, ident = consts[half]
        in_maps.append({
            "x": np.ascontiguousarray(x[b, half * T:(half + 1) * T, :]),
            "wq": wq, "wk": wk, "wv": wv, "wg": wg, "wo": wo,
            "w1s": np.ascontiguousarray(w1[:, c * 256:(c + 1) * 256]),
            "w2s": np.ascontiguousarray(w2[:, c * 256:(c + 1) * 256]),
            "w3s": np.ascontiguousarray(w3[:, c * 704:(c + 1) * 704]),
            "cqt": cqt, "sqt": sqt, "ckt": ckt, "skt": skt,
            "cmask": cmask, "scolt": scol,
            "gnc": gnc, "ident": ident,
        })
    res = run_bass_kernel_spmd(nc, in_maps, core_ids=list(range(8)))
    out = np.empty((4, 4096, D), np.float32)
    for c in range(8):
        b, half = c // 2, c % 2
        out[b, half * T:(half + 1) * T, :] = np.asarray(res.results[c]["y"], np.float32)
    return out


# version 20
# speedup vs baseline: 1.1887x; 1.0663x over previous
import numpy as np
import ml_dtypes
import concourse.bass as bass
import concourse.mybir as mybir
from concourse.alu_op_type import AluOpType as A
from concourse.bass_utils import run_bass_kernel_spmd
from contextlib import ExitStack

F32 = mybir.dt.float32
BF16 = mybir.dt.bfloat16
AF = mybir.ActivationFunctionType
AX = mybir.AxisListType

H, DK, DV, D, HF = 8, 256, 512, 2048, 5632
T = 2048
NT = 16
NCH = 8
FC = 16
SCALE = DK ** -0.5
ENGINES = ("sync", "gpsimd", "tensor", "vector", "scalar")


def _gammas():
    return 1.0 - 2.0 ** (-5.0 - np.arange(H, dtype=np.float64))


class Sem:
    def __init__(self, h, name):
        self.h, self.name, self.n = h, name, 0


class Prog:
    def __init__(self):
        self.p = {e: [] for e in ENGINES}

    def op(self, eng, method, *args, inc=None, incn=1, **kw):
        if inc is not None:
            inc.n += incn
        semh = inc.h if inc is not None else None

        def f(e, _m=method, _a=args, _k=kw, _s=semh, _n=incn):
            r = getattr(e, _m)(*_a, **_k)
            if _s is not None:
                r.then_inc(_s, _n)
        self.p[eng].append(f)
        return inc.n if inc is not None else None

    def wait(self, eng, sem, val=None):
        v = sem.n if val is None else val
        if v <= 0:
            return
        self.p[eng].append(lambda e, _s=sem.h, _v=v: e.wait_ge(_s, _v))


class Guard:

    def __init__(self):
        self.sem, self.val = None, 0

    def acquire(self, q, eng):
        if self.sem is not None:
            q.wait(eng, self.sem, self.val)

    def release(self, sem):
        self.sem, self.val = sem, sem.n


def build_program(debug_out=()):
    nc = bass.Bass("TRN2", target_bir_lowering=False, debug=False)

    def dram(name, shape, dtype, kind=None):
        if name in debug_out:
            kind = "ExternalOutput"
        if kind is None:
            return nc.dram_tensor(name, shape, dtype)
        return nc.dram_tensor(name, shape, dtype, kind=kind)

    x_d = dram("x", [T, D], F32, "ExternalInput").ap()
    wq_d = dram("wq", [D, D], F32, "ExternalInput").ap()
    wk_d = dram("wk", [D, D], F32, "ExternalInput").ap()
    wv_d = dram("wv", [D, 2 * D], F32, "ExternalInput").ap()
    wg_d = dram("wg", [D, 2 * D], F32, "ExternalInput").ap()
    wo_d = dram("wo", [2 * D, D], F32, "ExternalInput").ap()
    w1s_d = dram("w1s", [HF, 256], F32, "ExternalInput").ap()
    w2s_d = dram("w2s", [HF, 256], F32, "ExternalInput").ap()
    w3s_d = dram("w3s", [D, 704], F32, "ExternalInput").ap()
    cqt_d = dram("cqt", [128, H * T], BF16, "ExternalInput").ap()
    sqt_d = dram("sqt", [128, H * T], BF16, "ExternalInput").ap()
    ckt_d = dram("ckt", [128, H * T], BF16, "ExternalInput").ap()
    skt_d = dram("skt", [128, H * T], BF16, "ExternalInput").ap()
    cmask_d = dram("cmask", [128, 384], F32, "ExternalInput").ap()
    scol_d = dram("scolt", [128, H], F32, "ExternalInput").ap()
    gnc_d = dram("gnc", [128, 4], F32, "ExternalInput").ap()
    id_d = dram("ident", [128, 128], BF16, "ExternalInput").ap()
    y_d = dram("y", [T, D], F32, "ExternalOutput").ap()

    xn_d = dram("xn_h", [FC * 128, T], BF16).ap()
    q_d = dram("q_h", [128, NCH * 16 * 256], BF16).ap()
    k_d = dram("k_h", [128, NCH * 16 * 256], BF16).ap()
    ks_d = dram("ks_h", [T, H * DK], BF16).ap()
    v_d = dram("v_h", [T, H * DV], BF16).ap()
    sg_d = dram("sg_h", [T, H * DV], BF16).ap()
    ot_d = dram("ot_h", [128, NCH * 32 * 256], BF16).ap()
    at_d = dram("at_h", [T, D], BF16).ap()
    gin_d = dram("g_in", [12, 1], F32)
    gout_d = dram("g_out", [12, 1], F32)
    w1i_d = dram("w1qt_in", [256, HF], BF16)
    w2i_d = dram("w2qt_in", [256, HF], BF16)
    w3i_d = dram("w3qt_in", [704, D], BF16)
    w1t_d = dram("w1qt", [D, HF], BF16)
    w2t_d = dram("w2qt", [D, HF], BF16)
    w3t_d = dram("w3qt", [HF, D], BF16)
    si_d = dram("sp_in", [128, H * 1024], BF16)
    w1dbg_d = dram("w1qt_dbg", [D, HF], BF16)
    w1q_raw_d = dram("w1q_raw", [128, 11264], BF16)
    qsc_d = dram("qscale_dbg", [128, 16], F32)
    w1i_dbg_d = dram("w1i_dbg", [256, HF], BF16)
    w3dbg_d = dram("w3qt_dbg", [HF, D], BF16)
    lnst_d = dram("lnstat", [128, 9 * NT], F32)
    so_d = dram("sp_out", [256, H * 1024], BF16)

    q = Prog()
    es = ExitStack()
    E = es.enter_context

    ldK = Sem(E(nc.semaphore("ldK")), "ldK")
    ldA = Sem(E(nc.semaphore("ldA")), "ldA")
    ldB = Sem(E(nc.semaphore("ldB")), "ldB")
    ldC = Sem(E(nc.semaphore("ldC")), "ldC")
    ldD = Sem(E(nc.semaphore("ldD")), "ldD")
    cstA = Sem(E(nc.semaphore("cstA")), "cstA")
    cstB = Sem(E(nc.semaphore("cstB")), "cstB")
    stA = Sem(E(nc.semaphore("stA")), "stA")
    stB = Sem(E(nc.semaphore("stB")), "stB")
    tn = Sem(E(nc.semaphore("tn")), "tn")
    vc = Sem(E(nc.semaphore("vc")), "vc")
    sc = Sem(E(nc.semaphore("sc")), "sc")
    ccg = Sem(E(nc.semaphore("ccg")), "ccg")
    ccw1 = Sem(E(nc.semaphore("ccw1")), "ccw1")
    ccw2 = Sem(E(nc.semaphore("ccw2")), "ccw2")
    ccw3 = Sem(E(nc.semaphore("ccw3")), "ccw3")
    ccs = Sem(E(nc.semaphore("ccs")), "ccs")
    ALLS = [ldK, ldA, ldB, ldC, ldD, cstA, cstB, stA, stB, tn, vc, sc]

    def barrier():
        snap = [(s, s.n) for s in ALLS]
        for eng in ENGINES:
            for s, v in snap:
                q.wait(eng, s, v)

    id_s = E(nc.sbuf_tensor("id_s", [128, 128], BF16))
    gnc_s = E(nc.sbuf_tensor("gnc_s", [128, 4], F32))
    dumf = E(nc.sbuf_tensor("dumf", [128, 2], F32))
    q.op("sync", "dma_start", out=id_s[:, :], in_=id_d, inc=ldK, incn=16)
    q.op("sync", "dma_start", out=gnc_s[:, :], in_=gnc_d, inc=ldK, incn=16)

    rom2 = ExitStack()
    cmask_s = rom2.enter_context(nc.sbuf_tensor("cmask_s", [128, 384], F32))
    scol_s = rom2.enter_context(nc.sbuf_tensor("scol_s", [128, H], F32))
    q.op("sync", "dma_start", out=cmask_s[:, :], in_=cmask_d, inc=ldK, incn=16)
    q.op("sync", "dma_start", out=scol_s[:, :], in_=scol_d, inc=ldK, incn=16)
    consts_loaded = ldK.n
    gscope = ExitStack()
    parts = gscope.enter_context(nc.sbuf_tensor("parts", [128, 12], F32))
    onesc = gscope.enter_context(nc.sbuf_tensor("onesc", [128, 1], F32))
    ones1 = gscope.enter_context(nc.sbuf_tensor("ones1", [1, 128], F32))
    gs3 = gscope.enter_context(nc.sbuf_tensor("gs3", [12, 1], F32))
    gsr = gscope.enter_context(nc.sbuf_tensor("gsr", [1, 12], F32))
    g3 = gscope.enter_context(nc.sbuf_tensor("g3", [1, 3], F32))
    srow = gscope.enter_context(nc.sbuf_tensor("srow", [1, 3], F32))
    scm = gscope.enter_context(nc.sbuf_tensor("scm", [128, 3], F32))
    nscm = gscope.enter_context(nc.sbuf_tensor("nscm", [128, 3], F32))
    q.op("vector", "memset", onesc[:, :], 1.0)
    q.op("vector", "memset", ones1[:, :], 1.0, inc=vc)
    ones_v = vc.n
    mats = [
        (w1s_d, (44, 256), w1i_d, w1t_d, ccw1, "w12"),
        (w2s_d, (44, 256), w2i_d, w2t_d, ccw2, "w12"),
        (w3s_d, (16, 704), w3i_d, w3t_d, ccw3, "w3"),
    ]

    xnfm_scope = ExitStack()
    xn_fm = xnfm_scope.enter_context(nc.sbuf_tensor("xn_fm", [128, FC * T], BF16))
    xnv = xn_fm[:, :].rearrange("p (fc t) -> p fc t", fc=FC)

    a1a = ExitStack()
    x_s = [a1a.enter_context(nc.sbuf_tensor(f"x_s{i}", [128, D], F32, side="right")) for i in range(2)]
    xn_tm = [a1a.enter_context(nc.sbuf_tensor(f"xn_tm{i}", [128, D], BF16, side="right")) for i in range(2)]
    junkA = a1a.enter_context(nc.sbuf_tensor("junkA", [128, D], BF16, side="right"))
    sums = a1a.enter_context(nc.sbuf_tensor("sums", [128, NT], F32, side="right"))
    ssq = a1a.enter_context(nc.sbuf_tensor("ssq", [128, NT], F32, side="right"))
    sm1 = a1a.enter_context(nc.sbuf_tensor("sm1", [128, 6], F32, side="right"))
    smdbg = a1a.enter_context(nc.sbuf_tensor("smdbg", [128, 6 * NT], F32, side="right"))
    rstdall = a1a.enter_context(nc.sbuf_tensor("rstdall", [128, NT], F32, side="right"))
    bcol = a1a.enter_context(nc.sbuf_tensor("bcol", [128, 1], F32, side="right"))
    ptrA = a1a.enter_context(nc.psum_tensor("ptrA", [128, 1024], BF16))

    xg = [Guard(), Guard()]
    xng = [Guard(), Guard()]
    trg = [Guard(), Guard()]
    xn_ready = [0] * NT
    q.wait("sync", ldK, consts_loaded)
    for t in range(NT):
        bi = t % 2
        ldx = (ldA, ldB)[bi]
        xg[bi].acquire(q, "sync")
        q.op("sync", "dma_start", out=x_s[bi][:, :], in_=x_d[t * 128:(t + 1) * 128, :],
             inc=ldx, incn=16)
        lv = ldx.n
        q.wait("vector", ldx, lv)
        q.op("vector", "tensor_reduce", sums[:, t:t + 1], x_s[bi][:, :], AX.X, A.add, inc=vc)
        q.wait("scalar", ldx, lv)
        q.op("scalar", "activation", junkA[:, :], x_s[bi][:, :], AF.Square,
             accum_out=ssq[:, t:t + 1], inc=sc)
        q.wait("scalar", sc, sc.n)
        q.op("scalar", "copy", dumf[:, 0:1], ssq[:, t:t + 1], inc=sc)
        q.wait("vector", sc, sc.n)
        q.wait("vector", vc, vc.n)
        q.op("vector", "tensor_scalar_mul", sm1[:, 0:1], sums[:, t:t + 1], 1.0 / D, inc=vc)
        q.op("vector", "tensor_scalar", sm1[:, 1:2], ssq[:, t:t + 1], 1.0 / D, 1e-5,
             A.mult, A.add, inc=vc)
        q.wait("vector", vc, vc.n)
        q.op("vector", "tensor_scalar", sm1[:, 2:3], sm1[:, 0:1], sm1[:, 0:1], 0.0,
             A.mult, A.add, inc=vc)
        q.wait("vector", vc, vc.n)
        q.op("vector", "tensor_tensor", sm1[:, 3:4], sm1[:, 1:2], sm1[:, 2:3], A.subtract,
             inc=vc)
        q.wait("scalar", vc, vc.n)
        q.op("scalar", "activation", sm1[:, 4:5], sm1[:, 3:4], AF.Ln, inc=sc)
        q.wait("scalar", sc, sc.n)
        q.op("scalar", "activation", rstdall[:, t:t + 1], sm1[:, 4:5], AF.Exp, scale=-0.5, inc=sc)
        if "lnstat" in debug_out:
            q.wait("vector", sc, sc.n)
            q.op("vector", "tensor_copy", smdbg[:, t * 6:(t + 1) * 6], sm1[:, 0:6], inc=vc)
        q.wait("vector", sc, sc.n)
        q.op("vector", "scalar_tensor_tensor", bcol[:, :], sm1[:, 0:1], -1.0, rstdall[:, t:t + 1],
             A.mult, A.mult, inc=vc)
        xng[bi].acquire(q, "scalar")
        q.wait("scalar", vc, vc.n)
        q.op("scalar", "activation", xn_tm[bi][:, :], x_s[bi][:, :], AF.Identity,
             bias=bcol[:, 0:1], scale=rstdall[:, t:t + 1], inc=sc)
        xg[bi].release(sc)
        q.wait("tensor", sc, sc.n)
        for g in range(2):
            ptr = ptrA
            trg[0].acquire(q, "tensor")
            for j in range(8):
                fc = g * 8 + j
                q.op("tensor", "transpose", ptr[:, j * 128:(j + 1) * 128],
                     xn_tm[bi][:, fc * 128:(fc + 1) * 128], id_s[:, :],
                     inc=(tn if j == 7 else None))
            q.wait("vector", tn, tn.n)
            q.op("vector", "tensor_copy",
                 xnv[:, g * 8:(g + 1) * 8, t * 128:(t + 1) * 128],
                 ptr[:, :].rearrange("p (j t) -> p j t", j=8), inc=vc)
            trg[0].release(vc)
            if g == 1:
                xng[bi].release(vc)
                xn_ready[t] = vc.n
    if "lnstat" in debug_out:
        q.wait("scalar", vc, vc.n)
        q.wait("scalar", sc, sc.n)
        q.op("scalar", "dma_start", out=lnst_d.ap()[:, 0:NT], in_=sums[:, :], inc=stA, incn=16)
        q.op("scalar", "dma_start", out=lnst_d.ap()[:, NT:2 * NT], in_=ssq[:, :], inc=stA, incn=16)
        q.op("scalar", "dma_start", out=lnst_d.ap()[:, 2 * NT:3 * NT], in_=rstdall[:, :], inc=stA, incn=16)
        q.op("scalar", "dma_start", out=lnst_d.ap()[:, 3 * NT:9 * NT], in_=smdbg[:, :], inc=stA, incn=16)
    q.wait("scalar", vc, vc.n)
    for fc in range(FC):
        q.op("scalar", "dma_start", out=xn_d[fc * 128:(fc + 1) * 128, :],
             in_=xnv[:, fc, :], inc=stA, incn=16)

    qk = ExitStack()
    wqp = [qk.enter_context(nc.sbuf_tensor(f"wqp{i}", [128, 16 * 256], BF16, side="right")) for i in range(2)]
    wkp = [qk.enter_context(nc.sbuf_tensor(f"wkp{i}", [128, 16 * 256], BF16, side="right")) for i in range(2)]
    tbl = [qk.enter_context(nc.sbuf_tensor(f"tbl{i}", [128, 4 * 1024], BF16, side="right")) for i in range(2)]
    qbuf = [qk.enter_context(nc.sbuf_tensor(f"qbuf{i}", [128, 2816], F32, side="right")) for i in range(2)]
    qjunk = qk.enter_context(nc.sbuf_tensor("qjunk", [128, 2816], BF16, side="right"))
    q1s = qk.enter_context(nc.sbuf_tensor("q1s", [128, 1024], BF16, side="right"))
    q2s = qk.enter_context(nc.sbuf_tensor("q2s", [128, 1024], BF16, side="right"))
    rt1 = qk.enter_context(nc.sbuf_tensor("rt1", [128, 1024], BF16, side="right"))
    rt2 = qk.enter_context(nc.sbuf_tensor("rt2", [128, 1024], BF16, side="right"))
    qro = qk.enter_context(nc.sbuf_tensor("qro", [128, 2 * 1024], BF16, side="right"))
    kro = qk.enter_context(nc.sbuf_tensor("kro", [128, 2 * 1024], BF16, side="right"))
    kss = qk.enter_context(nc.sbuf_tensor("kss", [128, 8 * 256], BF16, side="right"))
    bufAq = qk.enter_context(nc.sbuf_tensor("bufAq", [128, 2816], BF16, side="right"))
    bufCq = qk.enter_context(nc.sbuf_tensor("bufCq", [128, 2816], BF16, side="right"))
    pP = [qk.enter_context(nc.psum_tensor(f"pP{i}", [128, 1024], F32)) for i in range(2)]
    ptrK = qk.enter_context(nc.psum_tensor("ptrK", [128, 1024], BF16))
    pq = qk.enter_context(nc.psum_tensor("pq", [128, 1], F32))
    ptrK2 = qk.enter_context(nc.psum_tensor("ptrK2", [128, 1024], BF16))

    wg_g = [Guard(), Guard()]
    p1jobs = []
    for m, (dr, (va, vb), *_r) in enumerate(mats):
        qa = va // 4
        for qi in range(4):
            p1jobs.append((m, qi, dr, qa))
    p1_loaded = [None] * 12
    qb_g = [Guard(), Guard()]

    def p1_load(j):
        m, qi, dr, qa = p1jobs[j]
        pbi = j % 2
        qb_g[pbi].acquire(q, "sync")
        srcv = dr.rearrange("(a p) f -> p a f", p=128)
        q.op("sync", "dma_start",
             out=qbuf[pbi][:, :].rearrange("p (a f) -> p a f", a=qa),
             in_=srcv[:, qi * qa:(qi + 1) * qa, :],
             inc=(ldC, ldD)[pbi], incn=16)
        p1_loaded[j] = (pbi, (ldC, ldD)[pbi].n)

    def p1_abs(j):
        m, qi, dr, qa = p1jobs[j]
        pbi, lv = p1_loaded[j]
        q.wait("scalar", (ldC, ldD)[pbi], lv)
        q.op("scalar", "activation", qjunk[:, :], qbuf[pbi][:, :], AF.Abs,
             accum_out=parts[:, m * 4 + qi: m * 4 + qi + 1], inc=sc)
        q.wait("scalar", sc, sc.n)
        q.op("scalar", "copy", dumf[:, 0:1], parts[:, m * 4 + qi:m * 4 + qi + 1], inc=sc)
        qb_g[pbi].release(sc)

    p1_done_sc = [0]
    p2_loaded = [None] * 8
    p2_quantv = [0] * 8
    p2s = {"tr_tn": 0, "stg_st": 0, "w1_stg": 0, "gB1_v": 0}
    ptrk2_g = Guard()

    def p2_load(j):
        m, qi = j // 4, j % 4
        pbi = j % 2
        qb_g[pbi].acquire(q, "sync")
        srcv = mats[m][0].rearrange("(a p) f -> p a f", p=128)
        q.op("sync", "dma_start",
             out=qbuf[pbi][:, :].rearrange("p (a f) -> p a f", a=11),
             in_=srcv[:, qi * 11:(qi + 1) * 11, :],
             inc=(ldC, ldD)[pbi], incn=16)
        p2_loaded[j] = (pbi, (ldC, ldD)[pbi].n)

    def p2_quant(j):
        m = j // 4
        pbi, lv = p2_loaded[j]
        q.wait("vector", (ldC, ldD)[pbi], lv)
        if j == 0:
            q.wait("vector", sc, p1_done_sc[0])
        q.wait("vector", stA, p2s["stg_st"])
        q.op("vector", "tensor_scalar", bufAq[:, :], qbuf[pbi][:, :], scm[:, m:m + 1],
             0.5, A.mult, A.is_ge, inc=vc)
        q.op("vector", "tensor_scalar", qjunk[:, :], qbuf[pbi][:, :], nscm[:, m:m + 1],
             0.5, A.mult, A.is_ge, inc=vc)
        qb_g[pbi].release(vc)
        q.wait("vector", vc, vc.n)
        q.wait("vector", tn, p2s["tr_tn"])
        q.op("vector", "tensor_tensor", bufCq[:, :], bufAq[:, :], qjunk[:, :],
             A.subtract, inc=vc)
        p2_quantv[j] = vc.n

    def p2_tr(j):
        m, qi = j // 4, j % 4
        gi = mats[m][2]
        wv_view = bufCq[:, :].rearrange("p (a f) -> p a f", a=11)
        stg = bufAq[:, :].rearrange("p (fc h) -> p fc h", fc=2)
        q.wait("tensor", vc, p2_quantv[j])
        for fc in range(2):
            for g0 in range(0, 11, 8):
                gn = min(8, 11 - g0)
                gidx = fc * 2 + g0 // 8
                bank = (ptrK, ptrK2)[gidx % 2]
                bgrd = (ptrk_g, ptrk2_g)[gidx % 2]
                bgrd.acquire(q, "tensor")
                for jj in range(gn):
                    q.op("tensor", "transpose", bank[:, jj * 128:(jj + 1) * 128],
                         wv_view[:, g0 + jj, fc * 128:(fc + 1) * 128], id_s[:, :],
                         inc=(tn if jj == gn - 1 else None))
                q.wait("vector", tn, tn.n)
                q.op("vector", "tensor_copy", stg[:, fc, g0 * 128:(g0 + gn) * 128],
                     bank[:, 0:gn * 128], inc=vc)
                bgrd.release(vc)
        p2s["tr_tn"] = tn.n
        q.wait("scalar", vc, vc.n)
        q.op("scalar", "dma_start",
             out=gi.ap().rearrange("(fc p) h -> p fc h", p=128)[:, :, qi * 1408:(qi + 1) * 1408],
             in_=stg[:, :, :], inc=stA, incn=16)
        p2s["stg_st"] = stA.n
        if j == 3:
            p2s["w1_stg"] = stA.n

    def gateA():
        q.wait("tensor", sc, p1_done_sc[0])
        q.wait("tensor", vc, ones_v)
        q.op("tensor", "matmul", pq[0:12, :], parts[:, :], onesc[:, :], start=True,
             stop=True, inc=tn)
        q.wait("vector", tn, tn.n)
        q.op("vector", "tensor_copy", gs3[:, :], pq[0:12, :], inc=vc)
        q.wait("scalar", vc, vc.n)
        q.op("scalar", "dma_start", out=gin_d.ap(), in_=gs3[:, :], inc=stA, incn=16)
        q.wait("gpsimd", stA, stA.n)
        q.op("gpsimd", "collective_compute", "AllReduce", A.add,
             replica_groups=[list(range(8))], ins=[gin_d.ap()], outs=[gout_d.ap()],
             inc=ccg)

    def gateB1():
        NW = float(HF * D)
        q.wait("gpsimd", ccg, 1)
        q.op("gpsimd", "dma_start", out=gsr[:, :],
             in_=gout_d.ap().rearrange("a b -> b a"), inc=cstA, incn=16)
        q.wait("vector", cstA, cstA.n)
        for m in range(3):
            q.op("vector", "tensor_reduce", g3[:, m:m + 1], gsr[:, 4 * m:4 * m + 4],
                 AX.X, A.add, inc=vc)
        q.wait("vector", vc, vc.n)
        q.op("vector", "tensor_scalar", srow[:, :], g3[:, :], 1.0 / NW, 1e-8,
             A.mult, A.add, inc=vc)
        q.wait("vector", vc, vc.n)
        q.op("vector", "reciprocal", srow[:, :], srow[:, :], inc=vc)
        p2s["gB1_v"] = vc.n

    def gateB2():
        for m in range(3):
            q.wait("tensor", vc, p2s["gB1_v"] if m == 0 else vc.n)
            q.op("tensor", "matmul", pq[:, :], ones1[:, :], srow[0:1, m:m + 1],
                 start=True, stop=True, inc=tn)
            q.wait("vector", tn, tn.n)
            q.op("vector", "tensor_copy", scm[:, m:m + 1], pq[:, :], inc=vc)
            q.wait("vector", vc, vc.n)
            q.op("vector", "tensor_scalar_mul", nscm[:, m:m + 1], scm[:, m:m + 1], -1.0,
                 inc=vc)

    def issue_ccw(m):
        gi, go, ccx = mats[m][2], mats[m][3], mats[m][4]
        q.wait("gpsimd", stA, p2s["w1_stg"] if m == 0 else p2s["stg_st"])
        q.op("gpsimd", "collective_compute", "AllGather", A.bypass,
             replica_groups=[list(range(8))], ins=[gi.ap()], outs=[go.ap()],
             inc=ccx)

    def run_slot(u):
        if u == 0:
            p1_load(0); p1_load(1)
        elif u <= 4:
            for j in range(3 * (u - 1), min(3 * u, 12)):
                p1_abs(j)
                if j == 11:
                    p1_done_sc[0] = sc.n
                if j + 2 < 12:
                    p1_load(j + 2)
        elif u == 5:
            gateA()
        elif u == 6:
            gateB1()
        elif u == 7:
            gateB2(); p2_load(0); p2_load(1)
        elif u == 8:
            p2_quant(0); p2_load(2)
        else:
            j = u - 9
            if j == 4:
                issue_ccw(0)
            p2_tr(j)
            if j + 1 <= 7:
                p2_quant(j + 1)
            if j + 3 <= 7:
                p2_load(j + 3)

    tblg = [Guard(), Guard()]
    tbl_ld = [0, 0]

    def load_tbl(u2):
        h2, th2 = u2 // 2, u2 % 2
        tbi = u2 % 2
        tblg[tbi].acquire(q, "sync")
        for ki, tab in enumerate((cqt_d, sqt_d, ckt_d, skt_d)):
            q.op("sync", "dma_start", out=tbl[tbi][:, ki * 1024:(ki + 1) * 1024],
                 in_=tab[:, h2 * T + th2 * 1024: h2 * T + (th2 + 1) * 1024],
                 inc=ldK, incn=16)
        tbl_ld[tbi] = ldK.n

    pg_g = [Guard(), Guard(), Guard()]
    robuf_g = Guard()
    krobuf_g = Guard()
    kss_g = Guard()
    ptrk_g = Guard()
    pend_ks = [None]

    def make_pend_ks(h, th, bi, rob_v):
        def pend():
            krov = kro[:, :].rearrange("p (dkc t) -> p dkc t", dkc=2)
            q.wait("tensor", vc, rob_v)
            kss_g.acquire(q, "tensor")
            for tt in range(8):
                bank = (ptrK, ptrK2)[tt // 4]
                bgrd = (ptrk_g, ptrk2_g)[tt // 4]
                if tt % 4 == 0:
                    bgrd.acquire(q, "tensor")
                for dkc in range(2):
                    q.op("tensor", "transpose",
                         bank[:, tt % 4 * 256 + dkc * 128: tt % 4 * 256 + (dkc + 1) * 128],
                         krov[:, dkc, tt * 128:(tt + 1) * 128], id_s[:, :],
                         inc=(tn if dkc == 1 else None))
                if tt % 4 == 3:
                    q.wait("vector", tn, tn.n)
                    q.op("vector", "tensor_copy",
                         kss[:, (tt - 3) * 256:(tt + 1) * 256],
                         bank[:, 0:1024], inc=vc)
                    bgrd.release(vc)
            krobuf_g.release(vc)
            q.wait("scalar", vc, vc.n)
            q.op("scalar", "dma_start",
                 out=ks_d[th * 1024:(th + 1) * 1024, h * 256:(h + 1) * 256].rearrange(
                     "(tt p) d -> p tt d", p=128),
                 in_=kss[:, :].rearrange("p (tt d) -> p tt d", tt=8), inc=stB, incn=16)
            kss_g.release(stB)
            if th == 1:
                wg_g[bi].release(tn)
        return pend

    def emit_rope(src_ps0, src_ps1, dst, th, cs, sn):
        q.op("scalar", "copy", q1s[:, :], src_ps0[:, :])
        q.op("scalar", "copy", q2s[:, :], src_ps1[:, :], inc=sc)
        q.wait("vector", sc, sc.n)
        q.op("vector", "tensor_tensor", rt1[:, :], q1s[:, :], cs, A.mult, inc=vc)
        q.op("vector", "tensor_tensor", rt2[:, :], q2s[:, :], sn, A.mult, inc=vc)
        q.wait("vector", vc, vc.n)
        q.op("vector", "tensor_tensor", dst[:, 0:1024], rt1[:, :], rt2[:, :], A.subtract,
             inc=vc)
        q.wait("vector", vc, vc.n)
        q.op("vector", "tensor_tensor", rt1[:, :], q2s[:, :], cs, A.mult, inc=vc)
        q.op("vector", "tensor_tensor", rt2[:, :], q1s[:, :], sn, A.mult, inc=vc)
        q.wait("vector", vc, vc.n)
        q.op("vector", "tensor_tensor", dst[:, 1024:2048], rt1[:, :], rt2[:, :], A.add,
             inc=vc)

    for h in range(H):
        bi = h % 2
        cstx = (cstA, cstB)[bi]
        wg_g[bi].acquire(q, "gpsimd")
        for half in range(2):
            q.op("gpsimd", "dma_start",
                 out=wqp[bi][:, :].rearrange("p (fc c) -> p fc c", fc=16)[:, half * 8:(half + 1) * 8, :],
                 in_=wq_d[:, h * 256:(h + 1) * 256].rearrange("(fc p) c -> p fc c", p=128)[:, half * 8:(half + 1) * 8, :],
                 inc=cstx, incn=16)
            q.op("gpsimd", "dma_start",
                 out=wkp[bi][:, :].rearrange("p (fc c) -> p fc c", fc=16)[:, half * 8:(half + 1) * 8, :],
                 in_=wk_d[:, h * 256:(h + 1) * 256].rearrange("(fc p) c -> p fc c", p=128)[:, half * 8:(half + 1) * 8, :],
                 inc=cstx, incn=16)
        panel_ld = cstx.n
        wqv = wqp[bi][:, :].rearrange("p (fc c) -> p fc c", fc=16)
        wkv = wkp[bi][:, :].rearrange("p (fc c) -> p fc c", fc=16)
        q.wait("tensor", cstx, panel_ld)
        for th in range(2):
            u = h * 2 + th
            tbi = u % 2
            if u == 0:
                load_tbl(0)
            if u + 1 < 16:
                load_tbl(u + 1)
            run_slot(u)
            q.wait("tensor", vc, xn_ready[th * 8 + 7])
            for dkc in range(2):
                pg_g[dkc].acquire(q, "tensor")
                for fc in range(FC):
                    for nh in range(2):
                        q.op("tensor", "matmul", pP[dkc][:, nh * 512:(nh + 1) * 512],
                             wqv[:, fc, dkc * 128:(dkc + 1) * 128],
                             xnv[:, fc, th * 1024 + nh * 512: th * 1024 + (nh + 1) * 512],
                             start=(fc == 0), stop=(fc == FC - 1),
                             inc=(tn if (fc == FC - 1 and nh == 1) else None))
            qmm_tn = tn.n
            if pend_ks[0] is not None:
                pend_ks[0](); pend_ks[0] = None
            q.wait("scalar", tn, qmm_tn)
            q.wait("vector", ldK, ldK.n)
            robuf_g.acquire(q, "scalar")
            emit_rope(pP[0], pP[1], qro, th,
                      tbl[tbi][:, 0:1024], tbl[tbi][:, 1024:2048])
            pg_g[0].release(sc)
            pg_g[1].release(sc)
            q.wait("scalar", vc, vc.n)
            qvw = q_d.rearrange("p (ch hd t) -> p ch hd t", ch=NCH, hd=16)
            qrv = qro[:, :].rearrange("p (dkc c4 t) -> p dkc c4 t", dkc=2, c4=4)
            for dkc in range(2):
                q.op("scalar", "dma_start",
                     out=qvw[:, th * 4:(th + 1) * 4, h * 2 + dkc, :],
                     in_=qrv[:, dkc, :, :], inc=stA, incn=16)
            robuf_g.release(stA)
            for dkc in range(2):
                pg_g[dkc].acquire(q, "tensor")
                for fc in range(FC):
                    for nh in range(2):
                        q.op("tensor", "matmul", pP[dkc][:, nh * 512:(nh + 1) * 512],
                             wkv[:, fc, dkc * 128:(dkc + 1) * 128],
                             xnv[:, fc, th * 1024 + nh * 512: th * 1024 + (nh + 1) * 512],
                             start=(fc == 0), stop=(fc == FC - 1),
                             inc=(tn if (fc == FC - 1 and nh == 1) else None))
            q.wait("scalar", tn, tn.n)
            krobuf_g.acquire(q, "scalar")
            robuf_g.acquire(q, "scalar")
            emit_rope(pP[0], pP[1], kro, th,
                      tbl[tbi][:, 2048:3072], tbl[tbi][:, 3072:4096])
            pg_g[0].release(sc)
            pg_g[1].release(sc)
            rob_v = vc.n
            tblg[tbi].release(vc)
            q.wait("scalar", vc, rob_v)
            kvw = k_d.rearrange("p (ch hd t) -> p ch hd t", ch=NCH, hd=16)
            krv = kro[:, :].rearrange("p (dkc c4 t) -> p dkc c4 t", dkc=2, c4=4)
            for dkc in range(2):
                q.op("scalar", "dma_start",
                     out=kvw[:, th * 4:(th + 1) * 4, h * 2 + dkc, :],
                     in_=krv[:, dkc, :, :], inc=stA, incn=16)
            robuf_g.release(stA)
            pend_ks[0] = make_pend_ks(h, th, bi, rob_v)
    if pend_ks[0] is not None:
        pend_ks[0](); pend_ks[0] = None
    p2_tr(7)
    issue_ccw(1)
    barrier()
    qk.close()
    a1a.close()

    vg = ExitStack()
    wvgp = vg.enter_context(nc.sbuf_tensor("wvgp", [128, 16 * 2048], BF16, side="right"))
    vst = [vg.enter_context(nc.sbuf_tensor(f"vst{i}", [128, 2048], BF16, side="right")) for i in range(2)]
    pV = [vg.enter_context(nc.psum_tensor(f"pV{i}", [128, 2048], F32)) for i in range(2)]
    pv_g = [Guard(), Guard()]
    vst_g = [Guard(), Guard()]
    qld = [0, 0, 0, 0]

    def vg_load(cb2, quar, gate_tn=None):
        src2 = wv_d if cb2 < 2 else wg_d
        coff2 = (cb2 % 2) * 2048
        if gate_tn is not None:
            q.wait("gpsimd", tn, gate_tn)
        q.op("gpsimd", "dma_start",
             out=wvgp[:, :].rearrange("p (fc c) -> p fc c", fc=16)[:, quar * 4:(quar + 1) * 4, :],
             in_=src2[:, coff2:coff2 + 2048].rearrange("(fc p) c -> p fc c", p=128)[:, quar * 4:(quar + 1) * 4, :],
             inc=cstA, incn=16)
        qld[quar] = cstA.n

    for quar in range(4):
        vg_load(0, quar)
    wvgv = wvgp[:, :].rearrange("p (fc c) -> p fc c", fc=16)
    for cb in range(4):
        coff = (cb % 2) * 2048
        lastq = [0, 0, 0, 0]
        for t in range(NT):
            bi = t % 2
            pv_g[bi].acquire(q, "tensor")
            for fc in range(FC):
                if t == 0 and fc == 0:
                    q.wait("tensor", cstA, qld[3])
                for nb in range(4):
                    q.op("tensor", "matmul", pV[bi][:, nb * 512:(nb + 1) * 512],
                         xnv[:, fc, t * 128:(t + 1) * 128],
                         wvgv[:, fc, nb * 512:(nb + 1) * 512],
                         start=(fc == 0), stop=(fc == FC - 1),
                         inc=(tn if (nb == 3 and fc % 4 == 3) else None))
                if t == NT - 1 and fc % 4 == 3:
                    lastq[fc // 4] = tn.n
            tv = tn.n
            if t == NT - 1 and cb < 3:
                for quar in range(4):
                    vg_load(cb + 1, quar, gate_tn=lastq[quar])
            if cb < 2:
                q.wait("vector", tn, tv)
                vst_g[bi].acquire(q, "vector")
                q.op("vector", "tensor_copy", vst[bi][:, :], pV[bi][:, :], inc=vc)
                pv_g[bi].release(vc)
                q.wait("scalar", vc, vc.n)
                q.op("scalar", "dma_start",
                     out=v_d[t * 128:(t + 1) * 128, coff:coff + 2048],
                     in_=vst[bi][:, :], inc=(stA, stB)[bi], incn=16)
                vst_g[bi].release((stA, stB)[bi])
            else:
                q.wait("scalar", tn, tv)
                vst_g[bi].acquire(q, "scalar")
                q.op("scalar", "activation", vst[bi][:, :], pV[bi][:, :], AF.Silu, inc=sc)
                pv_g[bi].release(sc)
                q.wait("scalar", sc, sc.n)
                q.op("scalar", "dma_start",
                     out=sg_d[t * 128:(t + 1) * 128, coff:coff + 2048],
                     in_=vst[bi][:, :], inc=(stA, stB)[bi], incn=16)
                vst_g[bi].release((stA, stB)[bi])
    barrier()
    vg.close()
    xnfm_scope.close()

    sscope = ExitStack()
    Sbuf = [sscope.enter_context(nc.sbuf_tensor(f"Sb{i}", [128, H * 1024], BF16))
            for i in range(2)]

    s1 = ExitStack()
    k1g = [s1.enter_context(nc.sbuf_tensor(f"k1g{i}", [128, 16 * 256], BF16, side="right")) for i in range(2)]
    v1g = [s1.enter_context(nc.sbuf_tensor(f"v1g{i}", [128, 16 * 512], BF16, side="right")) for i in range(2)]
    slb = s1.enter_context(nc.sbuf_tensor("slb", [128, H * 1024], BF16, side="right"))
    pS = [s1.enter_context(nc.psum_tensor(f"pS{i}", [128, 1024], F32)) for i in range(2)]
    ps_g = [Guard(), Guard()]
    str_g = [Guard(), Guard()]
    for h in range(H):
        bi = h % 2
        ldx = (ldA, ldB)[bi]
        str_g[bi].acquire(q, "sync")
        q.op("sync", "dma_start",
             out=k1g[bi][:, :].rearrange("p (tt d) -> p tt d", tt=16),
             in_=ks_d[:, h * 256:(h + 1) * 256].rearrange("(tt p) d -> p tt d", p=128),
             inc=ldx, incn=16)
        q.op("sync", "dma_start",
             out=v1g[bi][:, :].rearrange("p (tt d) -> p tt d", tt=16),
             in_=v_d[:, h * 512:(h + 1) * 512].rearrange("(tt p) d -> p tt d", p=128),
             inc=ldx, incn=16)
        sld = ldx.n
        k1v = k1g[bi][:, :].rearrange("p (tt d) -> p tt d", tt=16)
        v1v = v1g[bi][:, :].rearrange("p (tt d) -> p tt d", tt=16)
        q.wait("tensor", ldx, sld)
        ps_g[bi].acquire(q, "tensor")
        for tt in range(16):
            for dkc in range(2):
                q.op("tensor", "matmul", pS[bi][:, dkc * 512:(dkc + 1) * 512],
                     k1v[:, tt, dkc * 128:(dkc + 1) * 128],
                     v1v[:, tt, :],
                     start=(tt == 0), stop=(tt == 15),
                     inc=(tn if (tt == 15 and dkc == 1) else None))
        q.wait("vector", tn, tn.n)
        q.op("vector", "tensor_copy", slb[:, h * 1024:(h + 1) * 1024], pS[bi][:, :],
             inc=vc)
        ps_g[bi].release(vc)
        str_g[bi].release(tn)
    q.wait("scalar", vc, vc.n)
    q.op("scalar", "dma_start", out=si_d.ap(), in_=slb[:, :], inc=stA, incn=16)
    q.wait("gpsimd", stA, stA.n)
    q.op("gpsimd", "collective_compute", "AllGather", A.bypass,
         replica_groups=[[0, 1], [2, 3], [4, 5], [6, 7]],
         ins=[si_d.ap()], outs=[so_d.ap()], inc=ccs)
    s1.close()

    p0 = ExitStack()
    shard2 = [p0.enter_context(nc.sbuf_tensor(f"shard{i}", [128, 11264], F32, side="right"))
              for i in range(2)]
    bufA = p0.enter_context(nc.sbuf_tensor("bufA", [128, 11264], BF16, side="right"))
    bufB = p0.enter_context(nc.sbuf_tensor("bufB", [128, 11264], BF16, side="right"))
    bufC = p0.enter_context(nc.sbuf_tensor("bufC", [128, 11264], BF16, side="right"))
    ptr0 = p0.enter_context(nc.psum_tensor("ptr0", [128, 1024], BF16))

    shard_g = [Guard(), Guard()]
    shard_ld = [None, None, None]

    def p3_load(m):
        dr2 = mats[m][0]
        va2 = mats[m][1][0]
        mbi = m % 2
        ldw2 = (ldC, ldD)[mbi]
        srcv = dr2.rearrange("(a p) f -> p a f", p=128)
        half2 = va2 // 2
        shard_g[mbi].acquire(q, "sync")
        q.op("sync", "dma_start",
             out=shard2[mbi][:, :].rearrange("p (a f) -> p a f", a=va2)[:, 0:half2, :],
             in_=srcv[:, 0:half2, :], inc=ldw2, incn=16)
        q.op("sync", "dma_start",
             out=shard2[mbi][:, :].rearrange("p (a f) -> p a f", a=va2)[:, half2:va2, :],
             in_=srcv[:, half2:va2, :], inc=ldw2, incn=16)
        shard_ld[m] = (mbi, ldw2.n)

    q.wait("sync", tn, tn.n)
    q.wait("sync", vc, vc.n)
    q.wait("sync", stA, stA.n)
    p3_load(2)
    for m in [2]:
        dr, (va, vb), gi, go, ccx, kind = mats[m]
        mbi, mlv = shard_ld[m]
        shard = shard2[mbi]
        q.wait("vector", (ldC, ldD)[mbi], mlv)
        q.wait("vector", tn, tn.n)
        q.wait("vector", stA, stA.n)
        q.op("vector", "tensor_scalar", bufA[:, :], shard[:, :], scm[:, m:m + 1], 0.5,
             A.mult, A.is_ge, inc=vc)
        q.op("vector", "tensor_scalar", bufB[:, :], shard[:, :], nscm[:, m:m + 1], 0.5,
             A.mult, A.is_ge, inc=vc)
        shard_g[mbi].release(vc)
        q.wait("vector", vc, vc.n)
        q.op("vector", "tensor_tensor", bufC[:, :], bufA[:, :], bufB[:, :], A.subtract, inc=vc)
        wv_view = bufC[:, :].rearrange("p (a f) -> p a f", a=va)
        if kind == "w12":
            stg = bufA[:, :].rearrange("p (fc h) -> p fc h", fc=2)
            q.wait("tensor", vc, vc.n)
            for fc in range(2):
                for g0 in range(0, 44, 8):
                    gn = min(8, 44 - g0)
                    for j in range(gn):
                        q.op("tensor", "transpose", ptr0[:, j * 128:(j + 1) * 128],
                             wv_view[:, g0 + j, fc * 128:(fc + 1) * 128], id_s[:, :],
                             inc=(tn if j == gn - 1 else None))
                    q.wait("vector", tn, tn.n)
                    q.op("vector", "tensor_copy", stg[:, fc, g0 * 128:(g0 + gn) * 128],
                         ptr0[:, 0:gn * 128], inc=vc)
                    q.wait("tensor", vc, vc.n)
            q.wait("scalar", vc, vc.n)
            q.op("scalar", "dma_start", out=gi.ap().rearrange("(fc p) h -> p fc h", p=128),
                 in_=stg[:, :, :], inc=stA, incn=16)
        else:
            stg3_g = Guard()
            for hc in range(6):
                w = 128 if hc < 5 else 64
                stghc = bufA[:, 0:2048]
                q.wait("tensor", vc, vc.n)
                stg3_g.acquire(q, "vector")
                for g0 in range(0, 16, 8):
                    for j in range(8):
                        dd = g0 + j
                        q.op("tensor", "transpose", ptr0[0:w, j * 128:(j + 1) * 128],
                             wv_view[:, dd, hc * 128:hc * 128 + w], id_s[:, :],
                             inc=(tn if j == 7 else None))
                    q.wait("vector", tn, tn.n)
                    q.op("vector", "tensor_copy", stghc[0:w, g0 * 128:(g0 + 8) * 128],
                         ptr0[0:w, 0:1024], inc=vc)
                    q.wait("tensor", vc, vc.n)
                q.wait("scalar", vc, vc.n)
                q.op("scalar", "dma_start", out=gi.ap()[hc * 128:hc * 128 + w, :],
                     in_=stghc[0:w, :], inc=stA, incn=16)
                stg3_g.release(stA)
        if m == 0 and "w1q_raw" in debug_out:
            q.wait("scalar", vc, vc.n)
            q.op("scalar", "dma_start", out=w1q_raw_d.ap(), in_=bufC[:, :], inc=stA, incn=16)
            q.op("scalar", "dma_start", out=qsc_d.ap()[:, 0:3], in_=scm[:, :], inc=stA, incn=16, allow_slow_non_contiguous=True)
            q.op("scalar", "dma_start", out=qsc_d.ap()[:, 3:6], in_=nscm[:, :], inc=stA, incn=16, allow_slow_non_contiguous=True)
            q.wait("gpsimd", stA, stA.n)
            q.op("gpsimd", "dma_start", out=w1i_dbg_d.ap(), in_=gi.ap(), inc=cstA, incn=16)
        q.wait("gpsimd", stA, stA.n)
        q.op("gpsimd", "collective_compute", "AllGather", A.bypass,
             replica_groups=[list(range(8))], ins=[gi.ap()], outs=[go.ap()],
             inc=ccx)
    if "w1qt_dbg" in debug_out:
        q.wait("gpsimd", ccw1, 1)
        q.wait("gpsimd", ccw3, 1)
        q.op("gpsimd", "dma_start", out=w1dbg_d.ap(), in_=w1t_d.ap(), inc=cstA, incn=16)
        q.op("gpsimd", "dma_start", out=w3dbg_d.ap(), in_=w3t_d.ap(), inc=cstA, incn=16)
    sinscope = ExitStack()
    sin_b = sinscope.enter_context(nc.sbuf_tensor("sin_b", [128, H * 1024], BF16))
    q.wait("gpsimd", ccs, 1)
    q.op("gpsimd", "dma_start", out=sin_b[:, :], in_=so_d.ap()[0:128, :], inc=cstB, incn=16)
    q.wait("vector", cstB, cstB.n)
    for h in range(H):
        q.op("vector", "tensor_scalar_mul", Sbuf[0][:, h * 1024:(h + 1) * 1024],
             sin_b[:, h * 1024:(h + 1) * 1024], scol_s[:, h:h + 1], inc=vc)
    barrier()
    p0.close()
    sinscope.close()

    a2 = ExitStack()
    qsb = [a2.enter_context(nc.sbuf_tensor(f"qsb{i}", [128, 16 * 256], BF16, side="right")) for i in range(2)]
    ksb = [a2.enter_context(nc.sbuf_tensor(f"ksb{i}", [128, 16 * 256], BF16, side="right")) for i in range(2)]
    k2b = [a2.enter_context(nc.sbuf_tensor(f"k2b{i}", [128, 2 * 2048], BF16, side="right")) for i in range(2)]
    v2b = [a2.enter_context(nc.sbuf_tensor(f"v2b{i}", [128, 2 * 4096], BF16, side="right")) for i in range(2)]
    g2b = [a2.enter_context(nc.sbuf_tensor(f"g2b{i}", [128, 2 * 4096], BF16, side="right")) for i in range(2)]
    stsb = [a2.enter_context(nc.sbuf_tensor(f"stsb{i}", [128, 384], BF16, side="right"))
            for i in range(2)]
    oall = [a2.enter_context(nc.sbuf_tensor(f"oall{i}", [128, 16 * 512], BF16, side="right")) for i in range(2)]
    junk2 = a2.enter_context(nc.sbuf_tensor("junk2", [128, 512], BF16, side="right"))
    rsums = a2.enter_context(nc.sbuf_tensor("rsums", [128, 32], F32, side="right"))
    rmsc = a2.enter_context(nc.sbuf_tensor("rmsc", [128, 32], F32, side="right"))
    ost = a2.enter_context(nc.sbuf_tensor("ost", [128, 1024], BF16, side="right"))
    psA = [a2.enter_context(nc.psum_tensor(f"psA{i}", [128, 512], F32)) for i in range(2)]
    pOC = [a2.enter_context(nc.psum_tensor(f"pOC{i}", [128, 1024], F32)) for i in range(2)]
    pST = a2.enter_context(nc.psum_tensor("pST", [128, 512], F32))
    ptr2 = a2.enter_context(nc.psum_tensor("ptr2", [128, 1024], BF16))

    buf_g = [Guard(), Guard()]
    pST_g, ptr2_g = Guard(), Guard()
    psA_gv = [Guard(), Guard()]
    poc_g = [Guard(), Guard()]
    stsb_g, ost_g = Guard(), Guard()

    def load_chunk(ch):
        bi = ch % 2
        ldx = (ldA, ldB)[bi]
        buf_g[bi].acquire(q, "sync")
        c0 = ch * 256
        q.op("sync", "dma_start",
             out=qsb[bi][:, :],
             in_=q_d[:, ch * 4096:(ch + 1) * 4096],
             inc=ldx, incn=16)
        q.op("sync", "dma_start",
             out=ksb[bi][:, :],
             in_=k_d[:, ch * 4096:(ch + 1) * 4096],
             inc=ldx, incn=16)
        q.op("sync", "dma_start",
             out=k2b[bi][:, :].rearrange("p (tt d) -> p tt d", tt=2),
             in_=ks_d[c0:c0 + 256, :].rearrange("(tt p) d -> p tt d", p=128),
             inc=ldx, incn=16)
        q.op("sync", "dma_start",
             out=v2b[bi][:, :].rearrange("p (tt d) -> p tt d", tt=2),
             in_=v_d[c0:c0 + 256, :].rearrange("(tt p) d -> p tt d", p=128),
             inc=ldx, incn=16)
        q.op("sync", "dma_start",
             out=g2b[bi][:, :].rearrange("p (tt d) -> p tt d", tt=2),
             in_=sg_d[c0:c0 + 256, :].rearrange("(tt p) d -> p tt d", p=128),
             inc=ldx, incn=16)
        return (bi, ldx.n)

    chunk_ld = [None] * NCH
    chunk_ld[0] = load_chunk(0)

    pend = {"tail": None, "tr": []}

    def make_tail(ch, bi, sq_done, g2v):
        def emit():
            ro = bi * 16
            q.wait("vector", sc, sq_done)
            q.op("vector", "tensor_scalar", rmsc[:, ro:ro + 16], rsums[:, ro:ro + 16],
                 1.0 / DV, 1e-5, A.mult, A.add, inc=vc)
            q.wait("scalar", vc, vc.n)
            q.op("scalar", "activation", rmsc[:, ro:ro + 16], rmsc[:, ro:ro + 16],
                 AF.Ln, inc=sc)
            q.wait("scalar", sc, sc.n)
            q.op("scalar", "activation", rmsc[:, ro:ro + 16], rmsc[:, ro:ro + 16],
                 AF.Exp, scale=-0.5, inc=sc)
            q.wait("vector", sc, sc.n)
            for hh in range(H):
                for tp in range(2):
                    sl = oall[bi][:, (hh * 2 + tp) * 512:(hh * 2 + tp + 1) * 512]
                    q.op("vector", "scalar_tensor_tensor", sl, sl,
                         rmsc[:, ro + hh * 2 + tp: ro + hh * 2 + tp + 1],
                         g2v[:, tp, hh * 512:(hh + 1) * 512], A.mult, A.mult, inc=vc)
            return vc.n
        return emit

    def make_tr(ch, bi, hh):
        def emit(gate_v):
            q.wait("tensor", vc, gate_v)
            ptr2_g.acquire(q, "tensor")
            for dvc in range(4):
                for tp in range(2):
                    q.op("tensor", "transpose",
                         ptr2[:, dvc * 256 + tp * 128: dvc * 256 + (tp + 1) * 128],
                         oall[bi][:, (hh * 2 + tp) * 512 + dvc * 128:
                                  (hh * 2 + tp) * 512 + (dvc + 1) * 128],
                         id_s[:, :],
                         inc=(tn if (dvc == 3 and tp == 1) else None))
            q.wait("vector", tn, tn.n)
            ost_g.acquire(q, "vector")
            q.op("vector", "tensor_copy", ost[:, :], ptr2[:, :], inc=vc)
            ptr2_g.release(vc)
            q.wait("scalar", vc, vc.n)
            q.op("scalar", "dma_start",
                 out=ot_d[:, ch * 8192 + hh * 1024: ch * 8192 + (hh + 1) * 1024],
                 in_=ost[:, :],
                 inc=stA, incn=16)
            ost_g.release(stA)
        return emit

    def flush_pending():
        gate_v = None
        if pend["tail"] is not None:
            gate_v = pend["tail"]()
            pend["tail"] = None
        trs = pend["tr"]
        pend["tr"] = []
        return gate_v, trs

    for ch in range(NCH):
        bi = ch % 2
        Sp = Sbuf[ch % 2]
        Sn = Sbuf[(ch + 1) % 2]
        gate_v, trs = flush_pending()
        if gate_v is not None:
            buf_g[(ch - 1) % 2].sem, buf_g[(ch - 1) % 2].val = vc, gate_v
        if ch + 1 < NCH:
            chunk_ld[ch + 1] = load_chunk(ch + 1)
        qv = qsb[bi][:, :].rearrange("p (hd t) -> p hd t", hd=16)
        kv = ksb[bi][:, :].rearrange("p (hd t) -> p hd t", hd=16)
        k2v = k2b[bi][:, :].rearrange("p (tt d) -> p tt d", tt=2)
        v2v = v2b[bi][:, :].rearrange("p (tt d) -> p tt d", tt=2)
        g2v = g2b[bi][:, :].rearrange("p (tt d) -> p tt d", tt=2)
        q.wait("tensor", (ldA, ldB)[chunk_ld[ch][0]], chunk_ld[ch][1])
        prev_o = [None]
        for h in range(H):
            hs = h % 2
            psA_gv[hs].acquire(q, "tensor")
            for dkc in range(2):
                q.op("tensor", "matmul", psA[hs][:, 0:256],
                     kv[:, h * 2 + dkc, 0:128], qv[:, h * 2 + dkc, 0:256],
                     start=(dkc == 0), stop=(dkc == 1))
            for dkc in range(2):
                q.op("tensor", "matmul", psA[hs][:, 256:384],
                     kv[:, h * 2 + dkc, 128:256], qv[:, h * 2 + dkc, 128:256],
                     start=(dkc == 0), stop=(dkc == 1),
                     inc=(tn if dkc == 1 else None))
            sc_mm = tn.n
            q.wait("vector", tn, sc_mm)
            stsb_g.acquire(q, "vector")
            q.op("vector", "tensor_tensor", stsb[hs][:, 0:384], psA[hs][:, 0:384],
                 cmask_s[:, 0:384], A.mult, inc=vc)
            evv = vc.n
            psA_gv[hs].release(vc)
            pST_g.acquire(q, "tensor")
            for tt in range(2):
                q.op("tensor", "matmul", pST[:, :],
                     k2v[:, tt, h * 256: h * 256 + 128],
                     v2v[:, tt, h * 512:(h + 1) * 512],
                     start=(tt == 0), stop=(tt == 1),
                     inc=(tn if tt == 1 else None))
            st0 = tn.n
            if prev_o[0] is not None:
                prev_o[0]()
                prev_o[0] = None
            if trs:
                trs.pop(0)(gate_v)
            q.wait("vector", tn, st0)
            q.op("vector", "tensor_tensor", Sn[:, h * 1024:h * 1024 + 512],
                 Sp[:, h * 1024:h * 1024 + 512], pST[:, :], A.add, inc=vc)
            ev0 = vc.n
            pST_g.release(vc)
            pST_g.acquire(q, "tensor")
            for tt in range(2):
                q.op("tensor", "matmul", pST[:, :],
                     k2v[:, tt, h * 256 + 128: h * 256 + 256],
                     v2v[:, tt, h * 512:(h + 1) * 512],
                     start=(tt == 0), stop=(tt == 1),
                     inc=(tn if tt == 1 else None))
            st1 = tn.n
            q.wait("vector", tn, st1)
            q.op("vector", "tensor_tensor", Sn[:, h * 1024 + 512:(h + 1) * 1024],
                 Sp[:, h * 1024 + 512:(h + 1) * 1024], pST[:, :], A.add, inc=vc)
            pST_g.release(vc)

            def make_o(h=h, hs=hs, evv=evv, bi=bi, ch=ch, Sp=Sp,
                       qv=qv, v2v=v2v):
                def emit():
                    poc_g[hs].acquire(q, "tensor")
                    q.wait("tensor", vc, evv)
                    q.op("tensor", "matmul", pOC[hs][:, 0:512], stsb[hs][:, 0:128],
                         v2v[:, 0, h * 512:(h + 1) * 512], start=True, stop=False)
                    for dkc in range(2):
                        q.op("tensor", "matmul", pOC[hs][:, 0:512],
                             qv[:, h * 2 + dkc, 0:128],
                             Sp[:, (h * 2 + dkc) * 512:(h * 2 + dkc + 1) * 512],
                             start=False, stop=(dkc == 1))
                    q.op("tensor", "matmul", pOC[hs][:, 512:1024], stsb[hs][:, 128:256],
                         v2v[:, 0, h * 512:(h + 1) * 512], start=True, stop=False)
                    q.op("tensor", "matmul", pOC[hs][:, 512:1024], stsb[hs][:, 256:384],
                         v2v[:, 1, h * 512:(h + 1) * 512], start=False, stop=False)
                    for dkc in range(2):
                        q.op("tensor", "matmul", pOC[hs][:, 512:1024],
                             qv[:, h * 2 + dkc, 128:256],
                             Sp[:, (h * 2 + dkc) * 512:(h * 2 + dkc + 1) * 512],
                             start=False, stop=(dkc == 1),
                             inc=(tn if dkc == 1 else None))
                    omm = tn.n
                    q.wait("scalar", tn, omm)
                    q.op("scalar", "copy",
                         oall[bi][:, h * 1024:(h + 1) * 1024], pOC[hs][:, :], inc=sc)
                    poc_g[hs].release(sc)
                    q.wait("scalar", sc, sc.n)
                    for tp in range(2):
                        q.op("scalar", "activation", junk2[:, :],
                             oall[bi][:, (h * 2 + tp) * 512:(h * 2 + tp + 1) * 512],
                             AF.Square,
                             accum_out=rsums[:, bi * 16 + h * 2 + tp:
                                             bi * 16 + h * 2 + tp + 1], inc=sc)
                        q.wait("scalar", sc, sc.n)
                    q.op("scalar", "copy", dumf[:, 0:2],
                         rsums[:, bi * 16 + h * 2: bi * 16 + h * 2 + 2], inc=sc)
                return emit
            prev_o[0] = make_o()
        if prev_o[0] is not None:
            prev_o[0]()
            prev_o[0] = None
        for tr in trs:
            tr(gate_v)
        sq_done = sc.n
        pend["tail"] = make_tail(ch, bi, sq_done, g2v)
        pend["tr"] = [make_tr(ch, bi, hh) for hh in range(H)]
        buf_g[bi].sem, buf_g[bi].val = vc, vc.n
    gate_v, trs = flush_pending()
    for tr in trs:
        tr(gate_v)
    barrier()
    a2.close()
    sscope.close()
    gscope.close()
    rom2.close()

    a3 = ExitStack()
    wo_b = a3.enter_context(nc.sbuf_tensor("wo_b", [128, 32 * 2048], BF16, side="right"))
    otp = [a3.enter_context(nc.sbuf_tensor(f"otp{i}", [128, 32 * 256], BF16, side="right")) for i in range(2)]
    atb = [a3.enter_context(nc.sbuf_tensor(f"atb{i}", [128, 2048], BF16, side="right")) for i in range(2)]
    pW = [a3.enter_context(nc.psum_tensor(f"pW{i}", [128, 2048], F32)) for i in range(2)]
    wov = wo_b[:, :].rearrange("p (oc d) -> p oc d", oc=32)
    for g8 in range(8):
        q.op("gpsimd", "dma_start",
             out=wov[:, g8 * 4:(g8 + 1) * 4, :],
             in_=wo_d.rearrange("(oc p) d -> p oc d", p=128)[:, g8 * 4:(g8 + 1) * 4, :],
             inc=cstA, incn=16)
    wo_ld = cstA.n
    q.wait("vector", cstA, wo_ld)
    for oc in range(32):
        q.op("vector", "tensor_scalar_mul", wov[:, oc, :], wov[:, oc, :],
             gnc_s[:, oc % 4: oc % 4 + 1], inc=(vc if oc == 31 else None))
    wo_scaled = vc.n
    otp_g = [Guard(), Guard()]
    pw_g = [Guard(), Guard()]
    atb_g = [Guard(), Guard()]
    q.wait("tensor", vc, wo_scaled)
    for tpair in range(8):
        bi = tpair % 2
        ldx = (ldA, ldB)[bi]
        otp_g[bi].acquire(q, "sync")
        q.op("sync", "dma_start",
             out=otp[bi][:, :],
             in_=ot_d[:, tpair * 8192:(tpair + 1) * 8192],
             inc=ldx, incn=16)
        ot_ld = ldx.n
        otv = otp[bi][:, :].rearrange("p (ci ov t) -> p ci ov t", ci=1, ov=32)
        q.wait("tensor", ldx, ot_ld)
        for ti in range(2):
            tau = tpair * 2 + ti
            pw_g[ti].acquire(q, "tensor")
            for ov in range(32):
                for nb in range(4):
                    q.op("tensor", "matmul", pW[ti][:, nb * 512:(nb + 1) * 512],
                         otv[:, 0, ov, ti * 128:(ti + 1) * 128],
                         wov[:, ov, nb * 512:(nb + 1) * 512],
                         start=(ov == 0), stop=(ov == 31),
                         inc=(tn if (ov == 31 and nb == 3) else None))
            wmm = tn.n
            q.wait("vector", tn, wmm)
            atb_g[ti].acquire(q, "vector")
            q.op("vector", "tensor_copy", atb[ti][:, :], pW[ti][:, :], inc=vc)
            pw_g[ti].release(vc)
            q.wait("scalar", vc, vc.n)
            q.op("scalar", "dma_start", out=at_d[tau * 128:(tau + 1) * 128, :],
                 in_=atb[ti][:, :], inc=(stA, stB)[ti], incn=16)
            atb_g[ti].release((stA, stB)[ti])
            if ti == 1:
                otp_g[bi].release(tn)
    barrier()
    a3.close()

    bscope = ExitStack()
    hid = bscope.enter_context(nc.sbuf_tensor("hid", [128, 44 * 1024], BF16, side="right"))
    w3p0 = bscope.enter_context(nc.sbuf_tensor("w3p0", [128, 44 * 512], BF16, side="right"))
    w3_g = [Guard(), Guard()]
    hv = hid[:, :].rearrange("p (ht t) -> p ht t", ht=44)
    q.wait("sync", ccw1, 1)
    q.wait("sync", ccw2, 1)
    q.wait("sync", ccw3, 1)
    for ps in range(2):
        w3_ld2 = [0, 0]
        b1 = ExitStack()
        xnp = b1.enter_context(nc.sbuf_tensor(f"xnp{ps}", [128, 16 * 1024], BF16, side="right"))
        w1p = [b1.enter_context(nc.sbuf_tensor(f"w1p{ps}_{i}", [128, 16 * 256], BF16, side="right")) for i in range(2)]
        w2p = [b1.enter_context(nc.sbuf_tensor(f"w2p{ps}_{i}", [128, 16 * 256], BF16, side="right")) for i in range(2)]
        sgt = b1.enter_context(nc.sbuf_tensor(f"sgt{ps}", [128, 1024], BF16, side="right"))
        pX = [b1.enter_context(nc.psum_tensor(f"pX{ps}_{i}", [128, 1024], F32)) for i in range(4)]
        xnpv = xnp[:, :].rearrange("p (fc t) -> p fc t", fc=16)
        w3v0 = w3p0[:, :].rearrange("p (ht d) -> p ht d", ht=44)
        w3_g[0].acquire(q, "sync")
        for quar in range(4):
            q.op("sync", "dma_start", out=w3v0[:, quar * 11:(quar + 1) * 11, :],
                 in_=w3t_d.ap()[:, 0:512].rearrange(
                     "(ht p) d -> p ht d", p=128)[:, quar * 11:(quar + 1) * 11, :],
                 inc=ldD, incn=16)
        w3_ld2[0] = (ldD, ldD.n)
        for fc in range(FC):
            q.op("sync", "dma_start", out=xnpv[:, fc, :],
                 in_=xn_d[fc * 128:(fc + 1) * 128, ps * 1024:(ps + 1) * 1024],
                 inc=ldC, incn=16)
        xn_ld = ldC.n
        wp_g = [Guard(), Guard()]
        px_g = [Guard(), Guard(), Guard(), Guard()]
        sgt_g = Guard()
        hid_g = Guard()
        hid_g.acquire(q, "vector")
        for htp in range(22):
            bi = htp % 2
            ldx = (ldA, ldB)[bi]
            wp_g[bi].acquire(q, "sync")
            q.op("sync", "dma_start",
                 out=w1p[bi][:, :].rearrange("p (fc h) -> p fc h", fc=16),
                 in_=w1t_d.ap()[:, htp * 256:(htp + 1) * 256].rearrange("(fc p) h -> p fc h", p=128),
                 inc=ldx, incn=16)
            q.op("sync", "dma_start",
                 out=w2p[bi][:, :].rearrange("p (fc h) -> p fc h", fc=16),
                 in_=w2t_d.ap()[:, htp * 256:(htp + 1) * 256].rearrange("(fc p) h -> p fc h", p=128),
                 inc=ldx, incn=16)
            w_ld = ldx.n
            w1v = w1p[bi][:, :].rearrange("p (fc h) -> p fc h", fc=16)
            w2v = w2p[bi][:, :].rearrange("p (fc h) -> p fc h", fc=16)
            q.wait("tensor", ldx, w_ld)
            q.wait("tensor", ldC, xn_ld)
            for hh in range(2):
                ht = htp * 2 + hh
                p1i, p2i = (2 * bi), (2 * bi + 1)
                px_g[p1i].acquire(q, "tensor")
                px_g[p2i].acquire(q, "tensor")
                for fc in range(FC):
                    for nh in range(2):
                        q.op("tensor", "matmul", pX[p1i][:, nh * 512:(nh + 1) * 512],
                             w1v[:, fc, hh * 128:(hh + 1) * 128],
                             xnpv[:, fc, nh * 512:(nh + 1) * 512],
                             start=(fc == 0), stop=(fc == FC - 1))
                for fc in range(FC):
                    for nh in range(2):
                        q.op("tensor", "matmul", pX[p2i][:, nh * 512:(nh + 1) * 512],
                             w2v[:, fc, hh * 128:(hh + 1) * 128],
                             xnpv[:, fc, nh * 512:(nh + 1) * 512],
                             start=(fc == 0), stop=(fc == FC - 1),
                             inc=(tn if (fc == FC - 1 and nh == 1) else None))
                x_mm = tn.n
                q.wait("scalar", tn, x_mm)
                sgt_g.acquire(q, "scalar")
                q.op("scalar", "activation", sgt[:, :], pX[p1i][:, :], AF.Sigmoid, inc=sc)
                px_g[p1i].release(sc)
                q.wait("vector", sc, sc.n)
                q.op("vector", "tensor_tensor", hv[:, ht, :], sgt[:, :], pX[p2i][:, :],
                     A.mult, inc=vc)
                px_g[p2i].release(vc)
                sgt_g.release(vc)
            wp_g[bi].release(tn)
        b1_done_v = vc.n
        b1.close()
        q.wait("sync", tn, tn.n)
        q.wait("sync", vc, vc.n)
        q.wait("sync", sc, sc.n)
        b3 = ExitStack()
        w3p1 = b3.enter_context(nc.sbuf_tensor(f"w3p1_{ps}", [128, 44 * 512], BF16, side="right"))
        w3p = [w3p0, w3p1]
        xtb = [b3.enter_context(nc.sbuf_tensor(f"xtb{ps}_{i}", [128, 512], F32, side="right")) for i in range(2)]
        antb = [b3.enter_context(nc.sbuf_tensor(f"antb{ps}_{i}", [128, 512], BF16, side="right")) for i in range(2)]
        ob = [b3.enter_context(nc.sbuf_tensor(f"ob{ps}_{i}", [128, 512], F32, side="right")) for i in range(2)]
        pF = [b3.enter_context(nc.psum_tensor(f"pF{ps}_{i}", [128, 512], F32)) for i in range(2)]
        pf_g = [Guard(), Guard()]
        xt_g = [Guard(), Guard()]
        ob_g = [Guard(), Guard()]
        for dq in range(4):
            wbi = dq % 2
            ldw = (ldC, ldD)[wbi]
            w3v = w3p[wbi][:, :].rearrange("p (ht d) -> p ht d", ht=44)
            if dq >= 1:
                w3_g[wbi].acquire(q, "sync")
                for quar in range(4):
                    q.op("sync", "dma_start", out=w3v[:, quar * 11:(quar + 1) * 11, :],
                         in_=w3t_d.ap()[:, dq * 512:(dq + 1) * 512].rearrange(
                             "(ht p) d -> p ht d", p=128)[:, quar * 11:(quar + 1) * 11, :],
                         inc=ldw, incn=16)
                w3_ld = ldw.n
            else:
                ldw, w3_ld = w3_ld2[dq]
            for t in range(8):
                bi = t % 2
                tau = ps * 8 + t
                ldx = (ldA, ldB)[bi]
                xt_g[bi].acquire(q, "sync")
                q.op("sync", "dma_start", out=xtb[bi][:, :],
                     in_=x_d[tau * 128:(tau + 1) * 128, dq * 512:(dq + 1) * 512],
                     inc=ldx, incn=16)
                q.op("sync", "dma_start", out=antb[bi][:, :],
                     in_=at_d[tau * 128:(tau + 1) * 128, dq * 512:(dq + 1) * 512],
                     inc=ldx, incn=16)
                tile_ld = ldx.n
                pf_g[bi].acquire(q, "tensor")
                q.wait("tensor", ldw, w3_ld)
                q.wait("tensor", vc, b1_done_v)
                for ht in range(44):
                    q.op("tensor", "matmul", pF[bi][:, :], hv[:, ht, t * 128:(t + 1) * 128],
                         w3v[:, ht, :], start=(ht == 0), stop=(ht == 43),
                         inc=(tn if ht == 43 else None))
                f_mm = tn.n
                q.wait("vector", tn, f_mm)
                q.wait("vector", ldx, tile_ld)
                ob_g[bi].acquire(q, "vector")

                q.op("vector", "tensor_tensor", ob[bi][:, :], pF[bi][:, :], xtb[bi][:, :],
                     A.add, inc=vc)
                q.wait("vector", vc, vc.n)
                q.op("vector", "tensor_tensor", ob[bi][:, :], ob[bi][:, :], antb[bi][:, :],
                     A.add, inc=vc)
                pf_g[bi].release(vc)
                xt_g[bi].release(vc)
                q.wait("scalar", vc, vc.n)
                q.op("scalar", "dma_start",
                     out=y_d[tau * 128:(tau + 1) * 128, dq * 512:(dq + 1) * 512],
                     in_=ob[bi][:, :], inc=(stA, stB)[bi], incn=16)
                ob_g[bi].release((stA, stB)[bi])
            w3_g[wbi].release(tn)
        b3.close()
        if ps == 0:
            barrier()
    bscope.close()

    q.wait("scalar", stA, stA.n)
    q.wait("scalar", stB, stB.n)
    q.wait("sync", ldA, ldA.n)
    q.wait("sync", ldB, ldB.n)
    q.wait("sync", ldC, ldC.n)
    q.wait("gpsimd", cstA, cstA.n)
    q.wait("gpsimd", cstB, cstB.n)

    block = E(nc.Block())

    @block.sync
    def _(e):
        for f in q.p["sync"]:
            f(e)

    @block.gpsimd
    def _(e):
        for f in q.p["gpsimd"]:
            f(e)

    @block.tensor
    def _(e):
        for f in q.p["tensor"]:
            f(e)

    @block.vector
    def _(e):
        for f in q.p["vector"]:
            f(e)

    @block.scalar
    def _(e):
        for f in q.p["scalar"]:
            f(e)

    es.close()
    return nc


def _host_constants(half):
    g = _gammas()
    t0 = half * T
    tl = np.arange(T, dtype=np.float64)
    tabs = tl + t0
    invf = 10000.0 ** (-np.arange(0, DK, 2, dtype=np.float64) / DK)
    ang = tabs[None, :] * invf[:, None]
    c = np.cos(ang)
    s = np.sin(ang)
    cqt = np.zeros((128, H * T), np.float64)
    sqt = np.zeros((128, H * T), np.float64)
    ckt = np.zeros((128, H * T), np.float64)
    skt = np.zeros((128, H * T), np.float64)
    for h in range(H):
        lg = np.log(g[h])
        gq = np.exp(lg * (tl + 1.0)) * SCALE
        gk = np.exp(-lg * (tl + 1.0))
        cqt[:, h * T:(h + 1) * T] = c * gq[None, :]
        sqt[:, h * T:(h + 1) * T] = s * gq[None, :]
        ckt[:, h * T:(h + 1) * T] = c * gk[None, :]
        skt[:, h * T:(h + 1) * T] = s * gk[None, :]
    cmask = np.zeros((128, 384), np.float32)
    jj = np.arange(128)
    tril = (jj[None, :] >= jj[:, None]).astype(np.float32)
    cmask[:, 0:128] = tril
    cmask[:, 128:256] = 1.0
    cmask[:, 256:384] = tril
    scol = np.zeros((128, H), np.float32)
    if half == 1:
        for h in range(H):
            scol[:, h] = np.exp(np.log(g[h]) * 2048.0)
    ident = np.eye(128, dtype=ml_dtypes.bfloat16)
    bf = ml_dtypes.bfloat16
    return (cqt.astype(bf), sqt.astype(bf), ckt.astype(bf), skt.astype(bf),
            cmask, scol, ident)


_NC_CACHE = {}


def kernel(L0_attn_hidden_states, L0_attn_Wq, L0_attn_Wk, L0_attn_Wv, L0_attn_Wg,
           L0_attn_Wo, L0_attn_g_norm_weight, L0_ffn_w1, L0_ffn_w2, L0_ffn_w3):
    x = np.asarray(L0_attn_hidden_states, np.float32)
    wq = np.ascontiguousarray(np.asarray(L0_attn_Wq, np.float32))
    wk = np.ascontiguousarray(np.asarray(L0_attn_Wk, np.float32))
    wv = np.ascontiguousarray(np.asarray(L0_attn_Wv, np.float32))
    wg = np.ascontiguousarray(np.asarray(L0_attn_Wg, np.float32))
    wo = np.ascontiguousarray(np.asarray(L0_attn_Wo, np.float32))
    w1 = np.asarray(L0_ffn_w1, np.float32)
    w2 = np.asarray(L0_ffn_w2, np.float32)
    w3 = np.asarray(L0_ffn_w3, np.float32)
    gn = np.asarray(L0_attn_g_norm_weight, np.float32)
    gnc = np.ascontiguousarray(gn.reshape(4, 128).T)

    if "nc" not in _NC_CACHE:
        _NC_CACHE["nc"] = build_program()
    nc = _NC_CACHE["nc"]

    consts = [_host_constants(0), _host_constants(1)]
    in_maps = []
    for c in range(8):
        b, half = c // 2, c % 2
        cqt, sqt, ckt, skt, cmask, scol, ident = consts[half]
        in_maps.append({
            "x": np.ascontiguousarray(x[b, half * T:(half + 1) * T, :]),
            "wq": wq, "wk": wk, "wv": wv, "wg": wg, "wo": wo,
            "w1s": np.ascontiguousarray(w1[:, c * 256:(c + 1) * 256]),
            "w2s": np.ascontiguousarray(w2[:, c * 256:(c + 1) * 256]),
            "w3s": np.ascontiguousarray(w3[:, c * 704:(c + 1) * 704]),
            "cqt": cqt, "sqt": sqt, "ckt": ckt, "skt": skt,
            "cmask": cmask, "scolt": scol,
            "gnc": gnc, "ident": ident,
        })
    res = run_bass_kernel_spmd(nc, in_maps, core_ids=list(range(8)))
    out = np.empty((4, 4096, D), np.float32)
    for c in range(8):
        b, half = c // 2, c % 2
        out[b, half * T:(half + 1) * T, :] = np.asarray(res.results[c]["y"], np.float32)
    return out
